# version 11
# baseline (speedup 1.0000x reference)
import numpy as np
import ml_dtypes
from contextlib import ExitStack

import concourse.bass as bass
import concourse.mybir as mybir
from concourse.bass_utils import run_bass_kernel_spmd

F32 = mybir.dt.float32
F32R = mybir.dt.float32r
BF16 = mybir.dt.bfloat16
AF = mybir.ActivationFunctionType
ALU = mybir.AluOpType

D, Ff, NH, DH, Bb, Ls = 1024, 4096, 16, 64, 4, 4096
T = 2048
NC, NF = 8, 32
RANK = 16
LN_EPS = 1e-5
RMS_EPS = 1.1920929e-07
EPS_LIN = 1e-6
THETA = 10000.0
RMAGIC = 12582912.0

_STAGE = 99
_DBG = {}
_CACHE = {}


class P:
    ENGINES = ("te", "v", "s", "sy", "g")
    CHANNELS = ("syA", "syB", "syC", "syD", "gA", "gB")

    def __init__(self, nc):
        self.nc = nc
        self.q = {e: [] for e in self.ENGINES}
        self.cnt = {e: 0 for e in self.ENGINES}
        self.waited = {}
        self.sem = {}

    def init_sems(self, stack):
        for e in self.ENGINES + self.CHANNELS:
            self.sem[e] = stack.enter_context(self.nc.semaphore("sem_" + e))
            self.cnt.setdefault(e, 0)

    AUTOSYNC = ("v", "s", "g")

    def op(self, eng, fn, sig=0, sig_ch=None):
        ch = sig_ch or eng
        sem = self.sem[ch]
        if ch == eng and eng in self.AUTOSYNC:
            prev = self.cnt[eng]
            self.cnt[eng] += 1
            def run(e, fn=fn, sem=sem, prev=prev):
                e.wait_ge(sem, prev)
                fn(e).then_inc(sem, 1)
            self.q[eng].append(run)
            return self.cnt[eng]
        if sig:
            self.cnt[ch] += sig
            def run(e, fn=fn, sem=sem, sig=sig):
                fn(e).then_inc(sem, sig)
            self.q[eng].append(run)
            return self.cnt[ch]
        self.q[eng].append(lambda e, fn=fn: fn(e))
        return None

    def waitt(self, eng, token):
        if token:
            self.wait(eng, token[0], token[1])

    def wait(self, eng, src, ticket):
        if not ticket or ticket <= 0:
            return
        key = (eng, src)
        if self.waited.get(key, -1) >= ticket:
            return
        self.waited[key] = ticket
        sem = self.sem[src]
        self.q[eng].append(lambda e, sem=sem, t=ticket: e.wait_ge(sem, t))

    def barrier(self, skip=()):
        snap = dict(self.cnt)
        srcs = [s for s in self.ENGINES + self.CHANNELS
                if s not in skip and s not in ("sy", "g")]
        for w in self.ENGINES:
            if w in skip:
                continue
            for s in srcs:
                if w != s:
                    self.wait(w, s, snap[s])

    def run(self, block):
        q = self.q
        @block.sync
        def _(e):
            for fn in q["sy"]:
                fn(e)
        @block.tensor
        def _(e):
            for fn in q["te"]:
                fn(e)
        @block.vector
        def _(e):
            for fn in q["v"]:
                fn(e)
        @block.scalar
        def _(e):
            for fn in q["s"]:
                fn(e)
        @block.gpsimd
        def _(e):
            for fn in q["g"]:
                fn(e)


def _build(stage):
    nc = bass.Bass("TRN2", target_bir_lowering=False, debug=False)

    def din(name, shape, dt=F32):
        return nc.dram_tensor(name, shape, dt, kind="ExternalInput").ap()

    x_in = din("x", [T, D])
    wqkv_t = din("wqkv_t", [24, NC, 128, 128])
    wout_t = din("wout_t", [NC, 16, 64, 128])
    w1t = din("w1t", [D, Ff])
    w2t = din("w2t", [Ff, D])
    a_fc = din("a_fc", [RANK, D])
    b_fct = din("b_fct", [RANK, Ff])
    a_pj = din("a_pj", [RANK, Ff])
    b_pjt = din("b_pjt", [RANK, D])
    bias_cols = din("bias_cols", [128, 72])
    rows_f = din("rows_f", [1, 1024])
    rows_b = din("rows_b", [1, 2048])
    cosq = din("cosq", [128, T])
    sinq = din("sinq", [128, T])
    costm = din("costm", [128, 512])
    sintm = din("sintm", [128, 512])
    identf = din("identf", [128, 128])

    y_out = nc.dram_tensor("y", [T, D], F32, kind="ExternalOutput").ap()

    w1eff = nc.dram_tensor("w1eff", [NF * NC, 128, 128], BF16).ap()
    w2eff = nc.dram_tensor("w2eff", [NC * NF, 128, 128], BF16).ap()
    cc_in = nc.dram_tensor("cc_in", [64, NH * 65], F32).ap()
    cc_out = nc.dram_tensor("cc_out", [64, NH * 65], F32).ap()

    dbg_outs = {}
    def dbg(name, shape, dt=F32):
        if name in _DBG:
            dbg_outs[name] = nc.dram_tensor("dbg_" + name, shape, dt,
                                            kind="ExternalOutput").ap()
            return dbg_outs[name]
        return None

    with ExitStack() as stack:
        e_ = stack.enter_context
        p = P(nc)
        p.init_sems(stack)

        xres = e_(nc.sbuf_tensor("xres", [128, NC * T], F32))
        arena = e_(nc.sbuf_tensor("arena", [128, 58 * 1024], BF16))
        cosq_t = e_(nc.sbuf_tensor("cosq_t", [128, T], BF16))
        sinq_t = e_(nc.sbuf_tensor("sinq_t", [128, T], BF16))
        costm_t = e_(nc.sbuf_tensor("costm_t", [128, 512], BF16))
        sintm_t = e_(nc.sbuf_tensor("sintm_t", [128, 512], BF16))
        ident_t = e_(nc.sbuf_tensor("ident_t", [128, 128], F32))
        onesr = e_(nc.sbuf_tensor("onesr", [128, 128], F32R))
        onesb = e_(nc.sbuf_tensor("onesb", [128, 128], BF16))
        bias_t = e_(nc.sbuf_tensor("bias_t", [128, 72], F32))
        statr = e_(nc.sbuf_tensor("statr", [128, 64], F32))
        rowr = e_(nc.sbuf_tensor("rowr", [128, 2048], F32R))
        t1bt = e_(nc.sbuf_tensor("t1bt", [128, 1536], F32))
        psA = e_(nc.psum_tensor("psA", [128, 2048], F32))
        psB = e_(nc.psum_tensor("psB", [128, 2048], F32))

        def bank(i, w=1):
            t_ = psA if i < 4 else psB
            return t_[:, (i % 4) * 512:((i % 4) + w) * 512]

        A = arena

        def af32(off, n, part0=0, parts=128):
            return A[part0:part0 + parts, off:off + 2 * n].bitcast(F32)

        def abf(off, n, part0=0, parts=128):
            return A[part0:part0 + parts, off:off + n]

        def rrow(part, slot, n=512):
            off = 55296 + slot * 1024
            return A[part:part + 1, off:off + 2 * n].bitcast(F32)

        def rrow_r(part, slot, n=512):
            return rowr[part:part + 1, slot * 512:slot * 512 + n]

        def gbar():
            for s_ in ("te", "v", "s"):
                p.wait("g", s_, p.cnt[s_])

        def dma(qn, out, in_, ch=0, fence=True):
            chn = {"sy": ("syA", "syB"), "sy2": ("syC", "syD"),
                   "g": ("gA", "gB")}[qn]
            name = chn[ch % 2]
            qq = "sy" if qn.startswith("sy") else "g"
            if fence:
                p.wait(qq, name, p.cnt[name])
            t = p.op(qq, lambda e, o=out, i=in_: e.dma_start(out=o, in_=i),
                     sig=16, sig_ch=name)
            return (name, t)

        dma("g", cosq_t[:], cosq)
        dma("g", sinq_t[:], sinq)
        dma("g", costm_t[:], costm)
        dma("g", sintm_t[:], sintm)
        t_idl = dma("sy", ident_t[:], identf)
        t_bc = dma("sy", bias_t[:], bias_cols, ch=1)
        p.waitt("v", t_bc)
        p.op("g", lambda e: e.memset(onesb[:], 1.0))
        p.op("g", lambda e: e.memset(statr[:, 7:8], -RMAGIC))
        p.op("g", lambda e: e.memset(statr[:, 48:49], RMS_EPS))
        nc.const_aps.register(F32, RMS_EPS, statr[:, 48:49])
        p.op("g", lambda e: e.memset(statr[:, 49:50], EPS_LIN))
        nc.const_aps.register(F32, EPS_LIN, statr[:, 49:50])
        t_ms = p.op("g", lambda e: e.memset(statr[:, 6:7], 1.0), sig=1)
        p.wait("v", "g", t_ms)
        p.op("v", lambda e: e.tensor_copy(
            onesr[:], statr[:, 6:7].broadcast_to([128, 128])))
        p.op("v", lambda e: e.tensor_scalar(
            statr[:, 8:16], bias_t[:, 64:72], 0.5, None, ALU.mult), sig=1)

        xtm0 = af32(49152, 1024)
        xtm1 = af32(51200, 1024)
        t_ev = {}
        p.waitt("te", t_idl)
        for lc in range(16):
            buf = xtm0 if lc % 2 == 0 else xtm1
            if lc >= 2:
                p.wait("sy", "s", t_ev[(lc - 2) * NC + NC - 1])
            t_ld = dma("sy", buf, x_in[lc * 128:(lc + 1) * 128, :], ch=lc)
            p.waitt("te", t_ld)
            for c in range(NC):
                bk = (lc * NC + c) % 2
                if (lc * NC + c) >= 2:
                    p.wait("te", "s", t_ev[(lc * NC + c) - 2])
                p.op("te", lambda e, bk=bk, buf=buf, c=c: e.transpose(
                    bank(bk)[0:128, 0:128], buf[:, c * 128:(c + 1) * 128],
                    ident_t[:]), sig=1)
                p.wait("s", "te", p.cnt["te"])
                t_ev[lc * NC + c] = p.op("s", lambda e, bk=bk, c=c, lc=lc:
                    e.activation(
                        xres[:, c * T + lc * 128:c * T + (lc + 1) * 128],
                        bank(bk)[0:128, 0:128], AF.Copy), sig=1)

        def emit_wprep(wT, aT, bT, weff, kin, kout, guard=False,
                       t_redx=None, coissue=None):
            nkc = kin // 128
            av = af32(8192, kin, parts=16)
            bv = af32(16384, kout, parts=16)
            t_a = dma("sy", av, aT)
            t_b = dma("sy", bv, bT, ch=1)
            p.waitt("v", t_a)
            t_a2 = p.op("v", lambda e: e.tensor_scalar(
                av, av, 2.0, None, ALU.mult), sig=1)
            if t_redx is not None:
                t_red = t_redx
            else:
                if guard:
                    p.wait("g", "v", p.cnt["v"])
                    p.wait("g", "s", p.cnt["s"])
                pbufs = [af32(0, 4096), af32(24576, 4096),
                         af32(32768, 4096), af32(40960, 4096)]
                chq = [("sy", 0), ("sy", 1), ("g", 0), ("g", 1)]
                t_red = {}
                for i in range(nkc):
                    qn, cc = chq[i % 4]
                    if i >= 4:
                        p.wait(qn, "v", t_red[i - 4])
                    t_pl = dma(qn, pbufs[i % 4][:, 0:kout],
                               wT[i * 128:(i + 1) * 128, :], ch=cc)
                    p.waitt("v", t_pl)
                    t_red[i] = p.op("v",
                        lambda e, b=pbufs[i % 4], i=i, kout=kout:
                        e.reduce_max(
                            statr[:, 16 + i:17 + i], b[:, 0:kout],
                            axis=mybir.AxisListType.X,
                            apply_absolute_value=True), sig=1)
            p.wait("s", "v", t_red[nkc - 1])
            t_fl = p.op("s", lambda e, nkc=nkc: e.copy(
                statr[:, 16:16 + nkc], statr[:, 16:16 + nkc]), sig=1)
            p.wait("v", "s", t_fl)
            t_pm = p.op("v", lambda e, nkc=nkc: e.reduce_max(
                statr[:, 4:5], statr[:, 16:16 + nkc],
                axis=mybir.AxisListType.X), sig=1)
            p.wait("s", "v", t_pm)
            t_fl2 = p.op("s", lambda e: e.copy(statr[:, 4:5], statr[:, 4:5]),
                         sig=1)
            p.wait("te", "s", t_fl2)
            p.op("te", lambda e: e.transpose(
                bank(0)[0:1, 0:128], statr[:, 4:5], ident_t[:]), sig=1)
            p.wait("v", "te", p.cnt["te"])
            t_gm = p.op("v", lambda e: e.reduce_max(
                rrow(0, 0, 1), bank(0)[0:1, 0:128],
                axis=mybir.AxisListType.X), sig=1)
            p.wait("s", "v", t_gm)
            t_sc = p.op("s", lambda e: e.mul(
                rrow(0, 1, 1), rrow(0, 0, 1), 1.0 / 127.0), sig=1)
            p.wait("v", "s", t_sc)
            t_rc = p.op("v", lambda e: e.reciprocal(
                rrow(0, 2, 1), rrow(0, 1, 1)), sig=1)
            p.wait("s", "v", t_rc)
            t_fl3 = p.op("s", lambda e: e.copy(rrow(0, 2, 1), rrow(0, 2, 1)),
                         sig=1)
            p.wait("te", "s", t_fl3)
            p.op("te", lambda e: e.matmul(
                bank(0)[:, 0:1], onesr[0:1, :].bitcast(F32), rrow(0, 1, 1),
                start=True, stop=True))
            p.op("te", lambda e: e.matmul(
                bank(1)[:, 0:1], onesr[0:1, :].bitcast(F32), rrow(0, 2, 1),
                start=True, stop=True), sig=1)
            p.wait("s", "te", p.cnt["te"])
            p.op("s", lambda e: e.copy(statr[:, 2:3], bank(0)[:, 0:1]))
            t_stc = p.op("s", lambda e: e.copy(statr[:, 3:4], bank(1)[:, 0:1]),
                         sig=1)
            last01 = max(i for i in range(nkc) if i % 4 < 2)
            p.wait("sy", "v", t_red[last01])
            p.wait("s", "v", t_red[last01])
            p.wait("v", "s", t_stc)
            p.wait("te", "v", t_a2)
            p.waitt("te", t_b)

            pb = [af32(0, 512), af32(1024, 512), af32(26624, 512)]
            pq = [af32(2048, 512), af32(5120, 512), af32(28672, 512)]
            pqm = [af32(3072, 512), af32(6144, 512), af32(29696, 512)]
            wb = [abf(4096, 512), abf(4608, 512)]
            t_ld, t_ts, t_q, t_qm, t_st, t_lor = {}, {}, {}, {}, {}, {}
            nob = kout // 512
            for ob in range(nob):
                for i in range(nkc):
                    it = ob * nkc + i
                    if it >= 3:
                        p.wait("sy", "v", t_ts[it - 3])
                    t_ld[it] = dma("sy", pb[it % 3],
                                   wT[i * 128:(i + 1) * 128,
                                      ob * 512:(ob + 1) * 512], ch=it)
                    bkk = 2 + it % 2
                    if it >= 2:
                        p.wait("te", "v", t_q[it - 2])
                    t_lor[it] = p.op("te", lambda e, bkk=bkk, i=i, ob=ob:
                        e.matmul(bank(bkk),
                                 av[:, i * 128:(i + 1) * 128].bitcast(F32),
                                 bv[:, ob * 512:(ob + 1) * 512].bitcast(F32),
                                 start=True, stop=True), sig=1)
                    p.waitt("v", t_ld[it])
                    if it >= 3:
                        p.wait("v", "s", t_qm[it - 3])
                    t_ts[it] = p.op("v", lambda e, it=it: e.tensor_scalar(
                        pq[it % 3], pb[it % 3], statr[:, 3:4], RMAGIC,
                        ALU.mult, ALU.add), sig=1)
                    p.wait("s", "v", t_ts[it])
                    if it >= 3:
                        p.wait("s", "v", t_q[it - 3])
                    t_qm[it] = p.op("s", lambda e, it=it: e.activation(
                        pqm[it % 3], pq[it % 3], AF.Identity,
                        bias=statr[:, 7:8]), sig=1)
                    p.wait("v", "s", t_qm[it])
                    p.wait("v", "te", t_lor[it])
                    if it >= 2:
                        p.waitt("v", t_st[it - 2])
                    t_q[it] = p.op("v", lambda e, it=it, bkk=bkk:
                        e.scalar_tensor_tensor(
                            wb[it % 2], pqm[it % 3], statr[:, 2:3], bank(bkk),
                            ALU.mult, ALU.add), sig=1)
                    p.wait("g", "v", t_q[it])
                    for mm4 in range(4):
                        m = ob * 4 + mm4
                        t_st[it] = dma("g", weff[m * nkc + i],
                                       wb[it % 2][:, mm4 * 128:(mm4 + 1) * 128],
                                       ch=it, fence=(mm4 == 0))
                    if coissue is not None and it % 2 == 1:
                        coissue(it // 2)
            p.barrier(skip=("g",))

        if stage >= 1:
            co_tred = {}
            pbufs2 = [af32(32768, 1024), af32(34816, 1024)]

            def co_piece(jj):
                if jj == 0:
                    p.wait("sy", "v", p.cnt["v"])
                if jj >= 2:
                    p.wait("sy", "v", co_tred[jj - 2])
                t_pl = dma("sy2", pbufs2[jj % 2],
                           w2t[jj * 128:(jj + 1) * 128, :], ch=jj)
                p.waitt("v", t_pl)
                co_tred[jj] = p.op("v", lambda e, b=pbufs2[jj % 2], jj=jj:
                    e.reduce_max(statr[:, 16 + jj:17 + jj], b,
                                 axis=mybir.AxisListType.X,
                                 apply_absolute_value=True), sig=1)

            emit_wprep(w1t, a_fc, b_fct, w1eff, D, Ff, coissue=co_piece)
            dr = dbg("rows1", [1, 2048])
            if dr is not None:
                dma("sy", dr, A[0:1, 55296:59392].bitcast(F32))
                p.barrier(skip=("g",))
            d0 = dbg("statr1", [128, 64])
            if d0 is not None:
                dma("sy", d0, statr[:])
                p.barrier(skip=("g",))
            emit_wprep(w2t, a_pj, b_pjt, w2eff, Ff, D, t_redx=co_tred)
            d = dbg("w1eff0", [8, 128, 128], BF16)
            if d is not None:
                dma("sy", d, w1eff[0:8])
                p.barrier(skip=("g",))

        t1b = [t1bt[:, 0:512], t1bt[:, 512:1024], t1bt[:, 1024:1536]]

        def emit_ln(nblocks, lnb_slice, bs1, bs2, bab, bcb):
            xb = [abf(51200 + i * 512, 512) for i in range(4)]
            x2b = [abf(53248 + i * 512, 512) for i in range(4)]
            t_fin_prev = {0: 0, 1: 0}
            t_ap_prev = {0: 0, 1: 0}
            t_bc_prev = {0: 0, 1: 0}
            t_f4_prev = {0: 0, 1: 0}
            prev_s1mm = {0: {}, 1: {}}
            prev_s2mm = {0: {}, 1: {}}
            for inb, nb in enumerate(nblocks):
                par = inb % 2
                rp = 32 * par
                b1, b2, ba, bc = (bs1, bs2, bab, bcb) if par == 0 else \
                    (bs1 - 4, bs2 - 4, bab - 4, bcb - 4)
                onesr_p = onesr[rp:rp + 1, :]
                t_cast, t_sq, t_s1mm, t_s2mm = {}, {}, {}, {}
                p.wait("te", "v", t_fin_prev[par])
                for c in range(NC):
                    xs = xres[:, c * T + nb * 512:c * T + (nb + 1) * 512]
                    bi = 2 * par + (c % 2)
                    if c >= 2:
                        p.wait("v", "te", t_s1mm[c - 2])
                        p.wait("s", "te", t_s2mm[c - 2])
                    elif prev_s1mm[par]:
                        p.wait("v", "te", prev_s1mm[par][NC - 2 + c])
                        p.wait("s", "te", prev_s2mm[par][NC - 2 + c])
                    t_cast[c] = p.op("v", lambda e, xs=xs, bi=bi: e.tensor_copy(
                        xb[bi], xs), sig=1)
                    t_sq[c] = p.op("s", lambda e, xs=xs, bi=bi: e.activation(
                        x2b[bi], xs, AF.Square), sig=1)
                    p.wait("te", "v", t_cast[c])
                    p.wait("te", "s", t_sq[c])
                    t_s1mm[c] = p.op("te", lambda e, c=c, bi=bi, b1=b1: e.matmul(
                        bank(b1)[0:1, :], onesb[:, 0:1], xb[bi],
                        start=(c == 0), stop=(c == NC - 1)), sig=1)
                    t_s2mm[c] = p.op("te", lambda e, c=c, bi=bi, b2=b2: e.matmul(
                        bank(b2)[0:1, :], onesb[:, 0:1], x2b[bi],
                        start=(c == 0), stop=(c == NC - 1)), sig=1)
                p.wait("v", "te", t_s2mm[NC - 1])
                p.wait("v", "s", t_f4_prev[par])
                p.op("v", lambda e, rp=rp, b1=b1: e.tensor_scalar(
                    rrow(rp, 0), bank(b1)[0:1, :], 1.0 / D, None, ALU.mult))
                t_f1 = p.op("v", lambda e, rp=rp, b2=b2: e.tensor_scalar(
                    rrow(rp, 3), bank(b2)[0:1, :], 1.0 / D, LN_EPS, ALU.mult,
                    ALU.add), sig=1)
                t_fin_prev[par] = t_f1
                p.wait("s", "v", t_f1)
                t_f2 = p.op("s", lambda e, rp=rp: e.activation(
                    rrow(rp, 1), rrow(rp, 0), AF.Square), sig=1)
                p.wait("v", "s", t_f2)
                t_f3 = p.op("v", lambda e, rp=rp: e.tensor_sub(
                    rrow(rp, 3), rrow(rp, 3), rrow(rp, 1)), sig=1)
                p.wait("s", "v", t_f3)
                t_ln = p.op("s", lambda e, rp=rp: e.activation(
                    rrow(rp, 3), rrow(rp, 3), AF.Ln), sig=1)
                p.wait("v", "s", t_ln)
                t_cpv = p.op("v", lambda e, rp=rp: e.tensor_copy(
                    rrow(rp, 1), rrow(rp, 3)), sig=1)
                p.wait("s", "v", t_cpv)
                p.wait("s", "te", t_bc_prev[par])
                t_f4 = p.op("s", lambda e, rp=rp: e.activation(
                    rrow_r(rp, 1), rrow(rp, 1), AF.Exp, scale=-0.5), sig=1)
                t_f4_prev[par] = t_f4
                p.wait("v", "s", t_f4)
                p.wait("v", "te", t_bc_prev[par])
                t_f5 = p.op("v", lambda e, rp=rp: e.scalar_tensor_tensor(
                    rrow_r(rp, 2), rrow(rp, 0), -1.0,
                    rrow_r(rp, 1).bitcast(F32), ALU.mult, ALU.mult), sig=1)
                p.wait("te", "v", t_f5)
                p.wait("te", "v", t_ap_prev[par])
                p.op("te", lambda e, rp=rp, ba=ba, o_=onesr_p: e.matmul(
                    bank(ba), o_, rrow_r(rp, 1), start=True, stop=True))
                t_bc2 = p.op("te", lambda e, rp=rp, bc=bc, o_=onesr_p: e.matmul(
                    bank(bc), o_, rrow_r(rp, 2), start=True,
                    stop=True), sig=1)
                t_bc_prev[par] = t_bc2
                p.wait("v", "te", t_bc2)
                for c in range(NC):
                    xs = xres[:, c * T + nb * 512:c * T + (nb + 1) * 512]
                    p.op("v", lambda e, xs=xs, c=c, ba=ba: e.tensor_mul(
                        t1b[c % 3], xs, bank(ba)))
                    if c >= 1:
                        p.op("v", lambda e, c=c, nb=nb, bc=bc: e.tensor_add(
                            lnb_slice(c - 1, nb), t1b[(c - 1) % 3], bank(bc)))
                t_ap_prev[par] = p.op("v", lambda e, nb=nb, bc=bc: e.tensor_add(
                    lnb_slice(NC - 1, nb), t1b[(NC - 1) % 3], bank(bc)), sig=1)
                prev_s1mm[par], prev_s2mm[par] = t_s1mm, t_s2mm

        def emit_ffn():
            for H in (0, 1):
                def lnb_slice(c, nb, H=H):
                    return abf(32768 + c * 1024 + (nb - 2 * H) * 512, 512)
                emit_ln([2 * H, 2 * H + 1], lnb_slice, 4, 5, 6, 7)
                p.barrier(skip=("g",))
                if H == 0:
                    d = dbg("ln1b", [128, 8192], BF16)
                    if d is not None:
                        dma("sy", d, abf(32768, 8192))
                        p.barrier(skip=("g",))

                w1b = [abf(49152, 1024), abf(50176, 1024)]
                t_fcmm, t_gel = {}, {}
                for m in range(NF):
                    if m >= 2:
                        p.wait("sy", "te", t_fcmm[m - 2])
                    t_wld = dma("sy",
                                w1b[m % 2].rearrange("p (k c) -> p k c", k=8),
                                w1eff[m * 8:(m + 1) * 8]
                                .rearrange("k p c -> p k c"), ch=m)
                    p.waitt("te", t_wld)
                    if m >= 2:
                        p.wait("te", "s", t_gel[m - 2])
                    bk0 = 0 if m % 2 == 0 else 2
                    for k in range(NC):
                        for nb in range(2):
                            p.op("te", lambda e, bk0=bk0, k=k, nb=nb, m=m:
                                 e.matmul(bank(bk0 + nb),
                                          w1b[m % 2][:, k * 128:(k + 1) * 128],
                                          abf(32768 + k * 1024 + nb * 512, 512),
                                          start=(k == 0), stop=(k == NC - 1)),
                                 sig=(1 if (k == NC - 1 and nb == 1) else 0))
                    t_fcmm[m] = p.cnt["te"]
                    p.wait("s", "te", t_fcmm[m])
                    t_gel[m] = p.op("s", lambda e, bk0=bk0, m=m: e.activation(
                        abf(m * 1024, 1024), bank(bk0, 2), AF.Gelu,
                        bias=bias_t[:, 24 + m:25 + m]), sig=1)

                if H == 0:
                    d = dbg("hblk", [128, 8192], BF16)
                    if d is not None:
                        p.barrier(skip=("g",))
                        dma("sy", d, abf(0, 8192))
                        p.barrier(skip=("g",))
                w2b = [abf(40960, 4096), abf(45056, 4096)]
                t_pjmm, t_pj = {}, {}
                for m in range(NC):
                    if m >= 2:
                        p.wait("sy", "te", t_pjmm[m - 2])
                    t_w2 = dma("sy",
                               w2b[m % 2].rearrange("p (k c) -> p k c", k=32),
                               w2eff[m * 32:(m + 1) * 32]
                               .rearrange("k p c -> p k c"), ch=m)
                    p.waitt("te", t_w2)
                    if m >= 2:
                        p.wait("te", "v", t_pj[m - 2])
                    bk0 = 0 if m % 2 == 0 else 2
                    if m < 2:
                        p.wait("te", "s", t_gel[NF - 2 + m])
                    for k in range(NF):
                        p.wait("te", "s", t_gel[k])
                        for nb in range(2):
                            p.op("te", lambda e, bk0=bk0, k=k, nb=nb, m=m:
                                 e.matmul(bank(bk0 + nb),
                                          w2b[m % 2][:, k * 128:(k + 1) * 128],
                                          abf(k * 1024 + nb * 512, 512),
                                          start=(k == 0), stop=(k == NF - 1)),
                                 sig=(1 if (k == NF - 1 and nb == 1) else 0))
                    t_pjmm[m] = p.cnt["te"]
                    p.wait("v", "te", t_pjmm[m])
                    xs = xres[:, m * T + H * 1024:m * T + (H + 1) * 1024]
                    t_pj[m] = p.op("v", lambda e, bk0=bk0, xs=xs:
                        e.scalar_tensor_tensor(
                            xs, bank(bk0, 2), 0.5, xs, ALU.mult, ALU.add), sig=1)
                for m in range(NC):
                    p.wait("s", "v", t_pj[m])
                    xs = xres[:, m * T + H * 1024:m * T + (H + 1) * 1024]
                    p.op("s", lambda e, xs=xs, m=m: e.activation(
                        xs, xs, AF.Identity, bias=statr[:, 8 + m:9 + m]),
                        sig=(1 if m == NC - 1 else 0))
                p.barrier(skip=("g",))

        if stage >= 1:
            emit_ffn()

        def emit_attn():
            gbar()
            gkbc = af32(16384, 1024)
            bkbc = abf(18432, 1024)
            bvbc = abf(19456, 1024)
            rowbk = A[0:1, 20480:21504]
            rowbv = A[0:1, 21504:22528]
            vkacc = af32(22528, 1040, parts=64)
            t_rf = dma("sy", A[0:1, 55296:55296 + 2048].bitcast(F32), rows_f)
            t_rb = dma("g", A[0:1, 20480:22528], rows_b)
            p.waitt("v", t_rf)
            t_rr = p.op("v", lambda e: e.tensor_copy(
                rowr[0:1, 0:1024],
                A[0:1, 55296:55296 + 2048].bitcast(F32)), sig=1)
            p.wait("te", "v", t_rr)
            p.waitt("te", t_rb)
            for nb in range(2):
                p.op("te", lambda e, nb=nb: e.matmul(
                    bank(0 + nb), onesr[0:1, :],
                    rowr[0:1, nb * 512:(nb + 1) * 512],
                    start=True, stop=True))
                p.op("te", lambda e, nb=nb: e.matmul(
                    bank(2 + nb), onesb[0:1, :],
                    rowbk[0:1, nb * 512:(nb + 1) * 512], start=True, stop=True))
                p.op("te", lambda e, nb=nb: e.matmul(
                    bank(4 + nb), onesb[0:1, :],
                    rowbv[0:1, nb * 512:(nb + 1) * 512], start=True,
                    stop=True), sig=(1 if nb == 1 else 0))
            t_bcmm = p.cnt["te"]
            p.wait("v", "te", t_bcmm)
            p.op("v", lambda e: e.tensor_copy(gkbc, psA[:, 0:1024]))
            p.op("v", lambda e: e.tensor_copy(bkbc, psA[:, 1024:2048]))
            p.op("v", lambda e: e.tensor_copy(bvbc, psB[:, 0:1024]))
            p.op("v", lambda e: e.memset(vkacc, 0.0), sig=1)
            p.barrier(skip=("g",))

            def ln2_slice(c, nb):
                return abf(c * 2048 + nb * 512, 512)
            emit_ln([0, 1, 2, 3], ln2_slice, 4, 5, 6, 7)
            p.barrier(skip=("g",))
            d = dbg("ln2b", [128, 16384], BF16)
            if d is not None:
                dma("sy", d, abf(0, 16384))
                p.barrier(skip=("g",))

            wkvA = abf(32768, 4096)
            wkvB = abf(49152, 4096)
            k_sb = abf(36864, 2048)
            kf = [abf(38912 + i * 1024, 1024) for i in range(4)]
            ktl = abf(43008, 1024)
            u1 = abf(44032, 512)
            u2 = abf(44544, 512)
            vp = [abf(45056 + i * 1024, 1024) for i in range(4)]
            t_kvld, t_kvmm, t_kvev, t_rope, t_vkadd = {}, {}, {}, {}, {}
            t_vkmm2, t_k2d = {}, {}
            gbar()

            def prev_evkey(g, bb2, lq):
                if bb2 > 0:
                    return (g, bb2 - 1, lq)
                if g > 0:
                    return (g - 1, 3, lq)
                return None

            def emit_vk(g):
                for lq in range(2):
                    lc = 2 * g + lq
                    kfb = kf[lc % 4]
                    p.wait("te", "s", t_rope[lc])
                    p.wait("te", "v", t_kvev[(g, 3, lq)])
                    p.wait("te", "v", t_vkadd.get(lc - 1, 0))
                    for h in range(NH):
                        colv = [512 + 65 * h, 1024 + 65 * (h - 7),
                                1536 + 65 * (h - 14)][h // 7]
                        vks = psB[0:64, colv:colv + 65]
                        p.op("te", lambda e, h=h, kfb=kfb, lc=lc, vks=vks:
                             e.matmul(vks[:, 0:64],
                                      kfb[:, 64 * h:64 * h + 64],
                                      vp[lc % 4][:, 64 * h:64 * h + 64],
                                      start=True, stop=True), sig=1)
                        p.op("te", lambda e, h=h, kfb=kfb, vks=vks:
                             e.matmul(vks[:, 64:65],
                                      kfb[:, 64 * h:64 * h + 64],
                                      onesb[:, 0:1],
                                      start=True, stop=True), sig=1)
                    t_vkh = p.cnt["te"]
                    t_vkmm2[(g, lq)] = t_vkh
                    p.wait("v", "te", t_vkh)
                    p.op("v", lambda e: e.tensor_add(
                        vkacc[:, 0:455], vkacc[:, 0:455],
                        psB[0:64, 512:512 + 455]))
                    p.op("v", lambda e: e.tensor_add(
                        vkacc[:, 455:910], vkacc[:, 455:910],
                        psB[0:64, 1024:1024 + 455]))
                    t_vkadd[lc] = p.op("v", lambda e: e.tensor_add(
                        vkacc[:, 910:1040], vkacc[:, 910:1040],
                        psB[0:64, 1536:1536 + 130]), sig=1)

            for g in range(8):
                for bb2 in range(4):
                    it = g * 4 + bb2
                    wb = wkvA if it % 2 == 0 else wkvB
                    m0 = 8 + 4 * bb2
                    if it >= 2:
                        p.wait("g", "te", t_kvmm[it - 2])
                    tl = None
                    for kk in range(NC):
                        tl = dma("g",
                                 wb[:, kk * 512:(kk + 1) * 512]
                                 .rearrange("p (a c) -> p a c", a=4),
                                 wqkv_t[m0:m0 + 4, kk]
                                 .rearrange("a p c -> p a c"), ch=it,
                                 fence=(kk == 0))
                    t_kvld[it] = tl
                    p.waitt("te", t_kvld[it])
                    for kk in range(NC):
                        for lq in range(2):
                            lc = 2 * g + lq
                            if kk == 0:
                                pk = prev_evkey(g, bb2, lq)
                                if pk is not None:
                                    p.wait("te", "v", t_kvev.get(pk, 0))
                            p.op("te", lambda e, kk=kk, lq=lq, lc=lc, wb=wb:
                                 e.matmul(bank(lq),
                                          A[:, kk * 2048 + lc * 128:
                                            kk * 2048 + lc * 128 + 128],
                                          wb[:, kk * 512:(kk + 1) * 512],
                                          start=(kk == 0), stop=(kk == NC - 1)),
                                 sig=(1 if kk == NC - 1 else 0))
                    t_kvmm[it] = p.cnt["te"]
                    p.wait("v", "te", t_kvmm[it])
                    if bb2 < 2:
                        for lq in range(2):
                            if g >= 1 and bb2 == 0:
                                p.wait("v", "s", t_k2d.get((g - 1, lq), 0))
                            t_kvev[(g, bb2, lq)] = p.op("v",
                                lambda e, lq=lq, bb2=bb2: e.tensor_tensor(
                                    k_sb[:, lq * 1024 + bb2 * 512:
                                         lq * 1024 + bb2 * 512 + 512],
                                    bank(lq),
                                    bkbc[:, bb2 * 512:(bb2 + 1) * 512],
                                    ALU.add), sig=1)
                    if bb2 == 1:
                        for lq in range(2):
                            lc = 2 * g + lq
                            kfb = kf[lc % 4]
                            p.wait("s", "v", t_kvev[(g, 1, lq)])
                            if g >= 2:
                                p.wait("s", "te", t_vkmm2.get((g - 2, lq), 0))
                            t_k2 = p.op("s", lambda e, lq=lq, kfb=kfb:
                                e.activation(kfb,
                                             k_sb[:, lq * 1024:(lq + 1) * 1024],
                                             AF.Square,
                                             accum_out=statr[:, 0:1]), sig=1)
                            t_k2d[(g, lq)] = t_k2
                            p.op("s", lambda e: e.activation(
                                statr[:, 1:2], statr[:, 0:1], AF.Ln,
                                bias=RMS_EPS, scale=1.0 / D))
                            t_i2 = p.op("s", lambda e: e.activation(
                                statr[:, 1:2], statr[:, 1:2], AF.Exp,
                                scale=-0.5), sig=1)
                            p.wait("v", "s", t_i2)
                            p.op("v", lambda e, lq=lq: e.scalar_tensor_tensor(
                                ktl, k_sb[:, lq * 1024:(lq + 1) * 1024],
                                statr[:, 1:2], gkbc, ALU.mult, ALU.mult))
                            rp = lambda t_: t_.rearrange(
                                "p (h d) -> p h d", d=64)[:, :, 0:32]
                            ip = lambda t_: t_.rearrange(
                                "p (h d) -> p h d", d=64)[:, :, 32:64]
                            cpat = costm_t[:, lc * 32:(lc + 1) * 32] \
                                .unsqueeze(1).broadcast_to([128, 16, 32])
                            spat = sintm_t[:, lc * 32:(lc + 1) * 32] \
                                .unsqueeze(1).broadcast_to([128, 16, 32])
                            u1p = u1.rearrange("p (h d) -> p h d", d=32)
                            u2p = u2.rearrange("p (h d) -> p h d", d=32)
                            p.op("v", lambda e, cpat=cpat: e.tensor_tensor(
                                u1p, rp(ktl), cpat, ALU.mult))
                            p.op("v", lambda e, spat=spat: e.tensor_tensor(
                                u2p, rp(ktl), spat, ALU.mult))
                            p.op("v", lambda e, spat=spat, kfb=kfb:
                                 e.tensor_tensor(rp(kfb), ip(ktl), spat,
                                                 ALU.mult))
                            p.op("v", lambda e, kfb=kfb: e.tensor_tensor(
                                rp(kfb), u1p, rp(kfb), ALU.subtract))
                            p.op("v", lambda e, cpat=cpat, kfb=kfb:
                                 e.tensor_tensor(ip(kfb), ip(ktl), cpat,
                                                 ALU.mult))
                            t_rp6 = p.op("v", lambda e, kfb=kfb:
                                e.tensor_tensor(ip(kfb), u2p, ip(kfb),
                                                ALU.add), sig=1)
                            p.wait("s", "v", t_rp6)
                            t_rope[lc] = p.op("s", lambda e, kfb=kfb:
                                e.activation(kfb, kfb, AF.Relu), sig=1)
                    if bb2 >= 2:
                        for lq in range(2):
                            lc = 2 * g + lq
                            if g >= 2 and bb2 == 2:
                                p.wait("v", "te", t_vkmm2.get((g - 2, lq), 0))
                            t_kvev[(g, bb2, lq)] = p.op("v",
                                lambda e, lc=lc, bb2=bb2, lq=lq: e.tensor_tensor(
                                    vp[lc % 4][:, (bb2 - 2) * 512:(bb2 - 1) * 512],
                                    bank(lq),
                                    bvbc[:, (bb2 - 2) * 512:(bb2 - 1) * 512],
                                    ALU.add), sig=1)
                if g >= 1:
                    emit_vk(g - 1)
            emit_vk(7)
            t_a2_te = p.cnt["te"]
            d = dbg("kf15", [128, 1024], BF16)
            if d is not None:
                p.barrier(skip=("g",))
                dma("sy", d, kf[1])
                d2 = dbg("vp15", [128, 1024], BF16)
                dma("sy", d2, vp[1], ch=1)
                d3 = dbg("ksb15", [128, 1024], BF16)
                dma("sy", d3, k_sb[:, 1024:2048])
                p.barrier(skip=("g",))

            p.wait("sy", "v", t_vkadd[T // 128 - 1])
            d = dbg("vkacc", [64, 1040])
            if d is not None:
                dma("sy", d, vkacc)
            t_cci = dma("sy", cc_in, vkacc)
            p.waitt("g", t_cci)
            t_cc = p.op("g", lambda e: e.collective_compute(
                "AllReduce", ALU.add,
                replica_groups=[[0, 1], [2, 3], [4, 5], [6, 7]],
                ins=[cc_in.opt()], outs=[cc_out.opt()]), sig=1)
            vkrep = af32(53248, 520)
            vkrepb = abf(54288, 520)
            t_sc2 = {}
            for a in range(4):
                inr = cc_out[0:32, :].rearrange(
                    "p (j x) -> p j x", x=260)[:, :, 65 * a:65 * a + 65]
                ini = cc_out[32:64, :].rearrange(
                    "p (j x) -> p j x", x=260)[:, :, 65 * a:65 * a + 65]
                outr = vkrep[32 * a:32 * a + 32, 0:260].rearrange(
                    "p (j x) -> p j x", x=65)
                outi = vkrep[32 * a:32 * a + 32, 260:520].rearrange(
                    "p (j x) -> p j x", x=65)
                p.wait("g", "g", t_cc)
                dma("g", outr, inr, ch=1, fence=(a == 0))
                t_sc2[a] = dma("g", outi, ini, ch=1, fence=False)
            d = dbg("vkrep", [128, 520])
            if d is not None:
                p.waitt("sy", t_sc2[3])
                dma("sy", d, vkrep)

            wqb = [abf(32768, 1024), abf(33792, 1024)]
            q2b = [abf(34816, 1024), abf(35840, 1024)]
            wqs = [af32(36864, 1024), af32(38912, 1024)]
            t_qmm, t_qev, t_qst, t_cast = {}, {}, {}, {}
            p.waitt("s", t_cci)
            p.wait("sy", "te", t_a2_te)
            p.wait("sy", "s", t_k2d[(7, 1)])
            p.wait("sy", "v", p.cnt["v"])
            p.wait("v", "te", t_a2_te)
            for m in range(NC):
                if m >= 2:
                    p.wait("sy", "v", t_cast[m - 2])
                t_wq = dma("sy", wqs[m % 2].rearrange("p (k c) -> p k c", k=8),
                           wqkv_t[m].rearrange("k p c -> p k c"), ch=m)
                p.waitt("v", t_wq)
                if m >= 2:
                    p.wait("v", "te", t_qmm[(m - 2, 1)])
                t_cast[m] = p.op("v", lambda e, m=m: e.tensor_copy(
                    wqb[m % 2], wqs[m % 2]), sig=1)
                p.wait("te", "v", t_cast[m])
                for hf in range(2):
                    bk0 = 0 if (2 * m + hf) % 2 == 0 else 2
                    if m >= 1:
                        p.wait("te", "s", t_qev.get((m - 1, hf), 0))
                    for kk in range(NC):
                        for nb2 in range(2):
                            p.op("te", lambda e, bk0=bk0, kk=kk, nb2=nb2,
                                 hf=hf, m=m: e.matmul(
                                     bank(bk0 + nb2),
                                     wqb[m % 2][:, kk * 128:(kk + 1) * 128],
                                     abf(kk * 2048 + hf * 1024 + nb2 * 512, 512),
                                     start=(kk == 0), stop=(kk == NC - 1)),
                                 sig=(1 if kk == NC - 1 and nb2 == 1 else 0))
                    t_qmm[(m, hf)] = p.cnt["te"]
                    p.wait("s", "te", t_qmm[(m, hf)])
                    if m >= 1:
                        p.wait("s", "te", t_qst.get((m - 1, hf), 0))
                    p.op("s", lambda e, bk0=bk0, m=m, hf=hf: e.activation(
                        abf(16384 + m * 2048 + hf * 1024, 1024), bank(bk0, 2),
                        AF.Identity, bias=bias_t[:, m:m + 1]))
                    t_qev[(m, hf)] = p.op("s", lambda e, bk0=bk0, m=m, hf=hf:
                        e.activation(q2b[hf], bank(bk0, 2), AF.Square,
                                     bias=bias_t[:, m:m + 1]), sig=1)
                    p.wait("te", "s", t_qev[(m, hf)])
                    for nb2 in range(2):
                        p.op("te", lambda e, hf=hf, nb2=nb2, m=m: e.matmul(
                            bank(4 + 2 * hf + nb2)[0:1, :], onesb[:, 0:1],
                            q2b[hf][:, nb2 * 512:(nb2 + 1) * 512],
                            start=(m == 0), stop=(m == NC - 1)),
                            sig=(1 if nb2 == 1 else 0))
                    t_qst[(m, hf)] = p.cnt["te"]
            t_qstat = p.cnt["te"]
            p.waitt("v", t_sc2[3])
            t_vkb = p.op("v", lambda e: e.tensor_copy(vkrepb, vkrep), sig=1)

            p.wait("s", "te", t_qstat)
            p.op("s", lambda e: e.activation(
                rrow(32, 0, 2048), bank(4, 4)[0:1, :], AF.Ln,
                bias=RMS_EPS, scale=1.0 / D))
            t_inv3 = p.op("s", lambda e: e.activation(
                rowr[0:1, 0:2048], rrow(32, 0, 2048), AF.Exp,
                scale=-0.5), sig=1)

            scr1 = abf(32768, 2048)
            scr2 = abf(34816, 2048)
            scr3 = abf(36864, 2048)
            t_qrope = {}
            p.wait("v", "s", t_qev[(NC - 1, 1)])
            p.wait("v", "te", t_qstat)
            for c in range(4):
                qr = abf(16384 + c * 2048, 2048)
                qi = abf(16384 + (4 + c) * 2048, 2048)
                p.op("v", lambda e, qr=qr: e.tensor_tensor(
                    scr1, qr, cosq_t[:], ALU.mult))
                p.op("v", lambda e, qr=qr: e.tensor_tensor(
                    scr2, qr, sinq_t[:], ALU.mult))
                p.op("v", lambda e, qi=qi: e.tensor_tensor(
                    scr3, qi, sinq_t[:], ALU.mult))
                p.op("v", lambda e, qr=qr: e.tensor_tensor(
                    qr, scr1, scr3, ALU.subtract))
                p.op("v", lambda e, qi=qi: e.tensor_tensor(
                    scr3, qi, cosq_t[:], ALU.mult))
                t_r6 = p.op("v", lambda e, qi=qi: e.tensor_tensor(
                    qi, scr2, scr3, ALU.add), sig=1)
                p.wait("s", "v", t_r6)
                p.op("s", lambda e, qr=qr: e.activation(qr, qr, AF.Relu))
                t_qrope[c] = p.op("s", lambda e, qi=qi: e.activation(
                    qi, qi, AF.Relu), sig=1)
            p.wait("te", "s", t_inv3)
            p.wait("te", "s", t_qrope[3])
            for nb in range(4):
                p.op("te", lambda e, nb=nb: e.matmul(
                    bank(nb), onesr[0:1, :], rrow_r(0, nb),
                    start=True, stop=True), sig=(1 if nb == 3 else 0))
            t_invb = p.cnt["te"]
            p.wait("v", "te", t_invb)
            t_qsc = 0
            p.waitt("v", t_cci)
            for cc2 in range(8):
                for nb in range(4):
                    qs = abf(16384 + cc2 * 2048 + nb * 512, 512)
                    t_qsc = p.op("v", lambda e, qs=qs, nb=nb, cc2=cc2:
                        e.scalar_tensor_tensor(
                            qs, qs, bias_t[:, 8 + cc2:9 + cc2], bank(nb),
                            ALU.mult, ALU.mult),
                        sig=(1 if (cc2 == 7 and nb == 3) else 0))
            p.barrier(skip=("g",))
            d = dbg("qf", [128, 16384], BF16)
            if d is not None:
                dma("sy", d, abf(16384, 16384))
                p.barrier(skip=("g",))

            def attn_off(h):
                return h * 2048 if h < 8 else 32768 + (h - 8) * 2048

            t_evac, t_rcp, t_rbmm, t_div = {}, {}, {}, {}

            def emit_rb(i2):
                rbb = bank((4, 6)[i2 % 2], 2)[0:64, :]
                p.wait("te", "s", t_rcp[i2])
                if i2 >= 2:
                    p.wait("te", "v", t_div[i2 - 2])
                for nb2 in range(2):
                    p.op("te", lambda e, rbb=rbb, i2=i2, nb2=nb2: e.matmul(
                        rbb[:, nb2 * 512:(nb2 + 1) * 512],
                        onesr[64:65, 0:64],
                        rowr[64:65, (i2 % 2) * 1024 + nb2 * 512:
                             (i2 % 2) * 1024 + nb2 * 512 + 512],
                        start=True, stop=True, tile_position=(64, 0)),
                        sig=(1 if nb2 == 1 else 0))
                t_rbmm[i2] = p.cnt["te"]
                p.wait("v", "te", t_rbmm[i2])
                p.wait("v", "s", t_evac[i2])
                aoff = attn_off(i2 // 2) + (i2 % 2) * 1024
                t_div[i2] = p.op("v", lambda e, rbb=rbb, aoff=aoff:
                    e.tensor_tensor(
                        A[0:64, aoff:aoff + 1024],
                        A[0:64, aoff:aoff + 1024], rbb, ALU.mult), sig=1)

            p.wait("te", "v", t_vkb)
            for h in range(NH):
                a = h % 4
                j = h // 4
                tp = (96, 0) if a == 3 else None
                lreal = vkrepb[32 * a:32 * a + 32, 65 * j:65 * j + 65]
                limag = vkrepb[32 * a:32 * a + 32,
                               260 + 65 * j:260 + 65 * j + 65]
                for hf in range(2):
                    i2 = 2 * h + hf
                    bres = 0 if i2 % 2 == 0 else 2
                    res = bank(bres, 2)[0:65, :]
                    if i2 >= 2:
                        p.wait("te", "s", t_evac[i2 - 2])
                    for nb2 in range(2):
                        qoff = 16384 + (h // 4) * 2048 + hf * 1024 + nb2 * 512
                        qr = A[32 * a:32 * a + 32, qoff:qoff + 512]
                        qi = A[32 * a:32 * a + 32,
                               qoff + 8192:qoff + 8192 + 512]
                        p.op("te", lambda e, res=res, lreal=lreal, qr=qr,
                             tp=tp, nb2=nb2: e.matmul(
                                 res[:, nb2 * 512:(nb2 + 1) * 512], lreal, qr,
                                 start=True, stop=False, tile_position=tp))
                        p.op("te", lambda e, res=res, limag=limag, qi=qi,
                             tp=tp, nb2=nb2: e.matmul(
                                 res[:, nb2 * 512:(nb2 + 1) * 512], limag, qi,
                                 start=False, stop=True, tile_position=tp),
                             sig=(1 if nb2 == 1 else 0))
                    t_res = p.cnt["te"]
                    if i2 >= 1:
                        emit_rb(i2 - 1)
                    p.wait("s", "te", t_res)
                    if i2 >= 2:
                        p.wait("s", "te", t_rbmm[i2 - 2])
                    p.op("s", lambda e, res=res, i2=i2: e.activation(
                        rrow(64, 2 * (i2 % 2), 1024), res[64:65, :], AF.Ln,
                        bias=EPS_LIN))
                    t_rcp[i2] = p.op("s", lambda e, i2=i2: e.activation(
                        rowr[64:65, (i2 % 2) * 1024:(i2 % 2) * 1024 + 1024],
                        rrow(64, 2 * (i2 % 2), 1024), AF.Exp,
                        scale=-1.0), sig=1)
                    aoff = attn_off(h) + hf * 1024
                    t_evac[i2] = p.op("s", lambda e, res=res, aoff=aoff:
                        e.activation(A[0:64, aoff:aoff + 1024], res[0:64, :],
                                     AF.Copy), sig=1)
            emit_rb(2 * NH - 1)
            p.barrier(skip=("g",))
            d = dbg("attn0", [64, 4096], BF16)
            if d is not None:
                dma("sy", d, A[0:64, 0:4096])
                p.barrier(skip=("g",))

            wob = [abf(49152, 2048, parts=64), abf(51200, 2048, parts=64)]
            z2b = [abf(53248, 1024), abf(54272, 1024)]
            t_zmm, t_zev, t_zst = {}, {}, {}
            gbar()
            for m in range(NC):
                if m >= 2:
                    p.wait("g", "te", t_zmm[(m - 2, 1)])
                t_wo = dma("g", wob[m % 2].rearrange("p (k c) -> p k c", k=16),
                           wout_t[m].rearrange("k p c -> p k c"), ch=m)
                p.waitt("te", t_wo)
                for hf in range(2):
                    bk0 = 0 if (2 * m + hf) % 2 == 0 else 2
                    if m >= 1:
                        p.wait("te", "s", t_zev.get((m - 1, hf), 0))
                    for kk in range(16):
                        for nb2 in range(2):
                            p.op("te", lambda e, bk0=bk0, kk=kk, nb2=nb2,
                                 hf=hf, m=m: e.matmul(
                                     bank(bk0 + nb2),
                                     wob[m % 2][:, kk * 128:(kk + 1) * 128],
                                     A[0:64, attn_off(kk) + hf * 1024 +
                                       nb2 * 512:attn_off(kk) + hf * 1024 +
                                       nb2 * 512 + 512],
                                     start=(kk == 0), stop=(kk == 15)),
                                 sig=(1 if kk == 15 and nb2 == 1 else 0))
                    t_zmm[(m, hf)] = p.cnt["te"]
                    p.wait("s", "te", t_zmm[(m, hf)])
                    if m >= 1:
                        p.wait("s", "te", t_zst.get((m - 1, hf), 0))
                    p.op("s", lambda e, bk0=bk0, m=m, hf=hf: e.activation(
                        abf(16384 + m * 2048 + hf * 1024, 1024), bank(bk0, 2),
                        AF.Identity, bias=bias_t[:, 56 + m:57 + m]))
                    t_zev[(m, hf)] = p.op("s", lambda e, bk0=bk0, m=m, hf=hf:
                        e.activation(z2b[hf], bank(bk0, 2), AF.Square,
                                     bias=bias_t[:, 56 + m:57 + m]), sig=1)
                    p.wait("te", "s", t_zev[(m, hf)])
                    for nb2 in range(2):
                        p.op("te", lambda e, hf=hf, nb2=nb2, m=m: e.matmul(
                            bank(4 + 2 * hf + nb2)[0:1, :], onesb[:, 0:1],
                            z2b[hf][:, nb2 * 512:(nb2 + 1) * 512],
                            start=(m == 0), stop=(m == NC - 1)),
                            sig=(1 if nb2 == 1 else 0))
                    t_zst[(m, hf)] = p.cnt["te"]
            t_zstat = p.cnt["te"]
            p.wait("s", "te", t_zstat)
            p.op("s", lambda e: e.activation(
                rrow(32, 0, 2048), bank(4, 4)[0:1, :], AF.Ln,
                bias=RMS_EPS, scale=1.0 / D))
            t_inv4 = p.op("s", lambda e: e.activation(
                rowr[0:1, 0:2048], rrow(32, 0, 2048), AF.Exp,
                scale=-0.5), sig=1)
            p.wait("te", "s", t_inv4)
            for nb in range(4):
                p.op("te", lambda e, nb=nb: e.matmul(
                    bank(nb), onesr[0:1, :], rrow_r(0, nb),
                    start=True, stop=True), sig=(1 if nb == 3 else 0))
            t_invb2 = p.cnt["te"]
            p.wait("v", "te", t_invb2)
            t1z = af32(49152, 512)
            for cc2 in range(8):
                for nb in range(4):
                    zs = abf(16384 + cc2 * 2048 + nb * 512, 512)
                    xs = xres[:, cc2 * T + nb * 512:cc2 * T + (nb + 1) * 512]
                    p.op("v", lambda e, zs=zs, nb=nb, cc2=cc2:
                         e.scalar_tensor_tensor(
                             t1z, zs, bias_t[:, 16 + cc2:17 + cc2], bank(nb),
                             ALU.mult, ALU.mult))
                    p.op("v", lambda e, xs=xs: e.tensor_add(xs, xs, t1z),
                         sig=(1 if (cc2 == 7 and nb == 3) else 0))
            p.barrier(skip=("g",))

        if stage >= 2:
            emit_attn()

        if stage >= 3:
            emit_ffn()

        out0 = af32(49152, 1024)
        out1 = af32(51200, 1024)
        t_evo, t_sto = {}, {}
        for lc in range(16):
            buf = out0 if lc % 2 == 0 else out1
            for c in range(NC):
                bk = (lc * NC + c) % 2
                if (lc * NC + c) >= 2:
                    p.wait("te", "s", t_evo[(lc * NC + c) - 2])
                p.op("te", lambda e, bk=bk, c=c, lc=lc: e.transpose(
                    bank(bk)[0:128, 0:128],
                    xres[:, c * T + lc * 128:c * T + (lc + 1) * 128],
                    ident_t[:]), sig=1)
                t_mm = p.cnt["te"]
                if lc >= 2 and c == 0:
                    p.waitt("s", t_sto[lc - 2])
                p.wait("s", "te", t_mm)
                t_evo[lc * NC + c] = p.op("s", lambda e, bk=bk, buf=buf, c=c:
                    e.activation(buf[:, c * 128:(c + 1) * 128],
                                 bank(bk)[0:128, 0:128], AF.Copy), sig=1)
            p.wait("sy", "s", t_evo[lc * NC + NC - 1])
            t_sto[lc] = dma("sy", y_out[lc * 128:(lc + 1) * 128, :], buf, ch=lc)
        p.q["sy"].append(lambda e, t=p.cnt["syA"]: e.wait_ge(p.sem["syA"], t))
        p.q["sy"].append(lambda e, t=p.cnt["syB"]: e.wait_ge(p.sem["syB"], t))

        blk = e_(nc.Block())
        p.run(blk)
        return nc, dbg_outs


def _qperm():
    perm = np.zeros(D, np.int64)
    for c in range(8):
        for pp in range(128):
            h = 4 * (c % 4) + pp // 32
            j = pp % 32
            perm[c * 128 + pp] = 64 * h + 2 * j + (1 if c >= 4 else 0)
    return perm


def _kperm():
    perm = np.zeros(D, np.int64)
    for h in range(NH):
        for r in range(64):
            perm[64 * h + r] = 64 * h + (2 * r if r < 32 else 2 * (r - 32) + 1)
    return perm


def _host_prepare(inputs):
    x = np.asarray(inputs["L0_attn_x"], np.float32)
    wqkv = np.asarray(inputs["L0_attn_Wqkv"], np.float32)
    bqkv = np.asarray(inputs["L0_attn_bqkv"], np.float32)
    gq = np.asarray(inputs["L0_attn_gq"], np.float32)
    gk = np.asarray(inputs["L0_attn_gk"], np.float32)
    wout = np.asarray(inputs["L0_attn_Wout"], np.float32)
    bout = np.asarray(inputs["L0_attn_bout"], np.float32)
    gout = np.asarray(inputs["L0_attn_gout"], np.float32)
    wfc = np.asarray(inputs["L0_ffn_W_fc"], np.float32)
    bfc = np.asarray(inputs["L0_ffn_b_fc"], np.float32)
    afc = np.asarray(inputs["L0_ffn_A_fc"], np.float32)
    bfcm = np.asarray(inputs["L0_ffn_B_fc"], np.float32)
    wpj = np.asarray(inputs["L0_ffn_W_proj"], np.float32)
    bpj = np.asarray(inputs["L0_ffn_b_proj"], np.float32)
    apj = np.asarray(inputs["L0_ffn_A_proj"], np.float32)
    bpjm = np.asarray(inputs["L0_ffn_B_proj"], np.float32)

    qp = _qperm()
    kp = _kperm()
    wqkv_T = np.ascontiguousarray(wqkv.T)
    wqkv_p = np.concatenate([wqkv_T[:, 0:D][:, qp], wqkv_T[:, D:2 * D][:, kp],
                             wqkv_T[:, 2 * D:3 * D]], axis=1)
    wqkv_tiles = np.ascontiguousarray(
        wqkv_p.reshape(NC, 128, 24, 128).transpose(2, 0, 1, 3))
    wout_T = np.ascontiguousarray(wout.T)
    wout_tiles = np.ascontiguousarray(
        wout_T.reshape(16, 64, NC, 128).transpose(2, 0, 1, 3))

    bq_p = bqkv[0:D][qp]
    bk_p = bqkv[D:2 * D][kp]
    bv = bqkv[2 * D:3 * D]
    bias_cols = np.zeros((128, 72), np.float32)
    bias_cols[:, 0:8] = bq_p.reshape(8, 128).T
    bias_cols[:, 8:16] = gq[qp].reshape(8, 128).T
    bias_cols[:, 16:24] = gout.reshape(8, 128).T
    bias_cols[:, 24:56] = bfc.reshape(32, 128).T
    bias_cols[:, 56:64] = bout.reshape(8, 128).T
    bias_cols[:, 64:72] = bpj.reshape(8, 128).T

    rows_f = gk[kp].reshape(1, D)
    rows_b = np.zeros((1, 2048), np.float32)
    rows_b[0, 0:1024] = bk_p
    rows_b[0, 1024:2048] = bv

    j = np.arange(32, dtype=np.float64)
    invf = THETA ** (-j / 32.0)

    def tables(l0):
        t = np.arange(l0, l0 + T, dtype=np.float64)
        ang = np.outer(t, invf)
        cos = np.cos(ang).astype(np.float32)
        sin = np.sin(ang).astype(np.float32)
        cq = np.ascontiguousarray(np.tile(cos.T, (4, 1)))
        sq = np.ascontiguousarray(np.tile(sin.T, (4, 1)))
        ctm = np.zeros((128, 512), np.float32)
        stm = np.zeros((128, 512), np.float32)
        for lc in range(16):
            ctm[:, lc * 32:(lc + 1) * 32] = cos[lc * 128:(lc + 1) * 128, :]
            stm[:, lc * 32:(lc + 1) * 32] = sin[lc * 128:(lc + 1) * 128, :]
        return cq, sq, ctm, stm

    shared = {
        "wqkv_t": wqkv_tiles, "wout_t": wout_tiles,
        "w1t": np.ascontiguousarray(wfc.T), "w2t": np.ascontiguousarray(wpj.T),
        "a_fc": afc, "b_fct": np.ascontiguousarray(bfcm.T),
        "a_pj": apj, "b_pjt": np.ascontiguousarray(bpjm.T),
        "bias_cols": bias_cols, "rows_f": rows_f, "rows_b": rows_b,
        "identf": np.eye(128, dtype=np.float32),
    }
    in_maps = []
    for c in range(8):
        b = c // 2
        l0 = (c % 2) * T
        cq, sq, ctm, stm = tables(l0)
        m = dict(shared)
        m["x"] = np.ascontiguousarray(x[b, l0:l0 + T, :])
        m["cosq"], m["sinq"], m["costm"], m["sintm"] = cq, sq, ctm, stm
        in_maps.append(m)
    return in_maps


def kernel(**inputs):
    key = (_STAGE, tuple(sorted(_DBG)))
    if key not in _CACHE:
        _CACHE[key] = _build(_STAGE)
    nc, dbg_outs = _CACHE[key]
    in_maps = _host_prepare(inputs)
    res = run_bass_kernel_spmd(nc, in_maps, core_ids=list(range(8)))
    out = np.zeros((Bb, Ls, D), np.float32)
    for c in range(8):
        out[c // 2, (c % 2) * T:(c % 2 + 1) * T, :] = res.results[c]["y"]
    if _DBG:
        kernel._dbg_results = [
            {k: res.results[c].get("dbg_" + k) for k in _DBG} for c in range(8)]
    return out


# version 12 (final)
# speedup vs baseline: 1.0124x; 1.0124x over previous
import numpy as np
import ml_dtypes
from contextlib import ExitStack

import concourse.bass as bass
import concourse.mybir as mybir
from concourse.bass_utils import run_bass_kernel_spmd

F32 = mybir.dt.float32
F32R = mybir.dt.float32r
BF16 = mybir.dt.bfloat16
AF = mybir.ActivationFunctionType
ALU = mybir.AluOpType

D, Ff, NH, DH, Bb, Ls = 1024, 4096, 16, 64, 4, 4096
T = 2048
NC, NF = 8, 32
RANK = 16
LN_EPS = 1e-5
RMS_EPS = 1.1920929e-07
EPS_LIN = 1e-6
THETA = 10000.0
RMAGIC = 12582912.0

_STAGE = 99
_DBG = {}
_CACHE = {}


class P:
    ENGINES = ("te", "v", "s", "sy", "g")
    CHANNELS = ("syA", "syB", "syC", "syD", "gA", "gB")

    def __init__(self, nc):
        self.nc = nc
        self.q = {e: [] for e in self.ENGINES}
        self.cnt = {e: 0 for e in self.ENGINES}
        self.waited = {}
        self.sem = {}

    def init_sems(self, stack):
        for e in self.ENGINES + self.CHANNELS:
            self.sem[e] = stack.enter_context(self.nc.semaphore("sem_" + e))
            self.cnt.setdefault(e, 0)

    AUTOSYNC = ("v", "s", "g")

    def op(self, eng, fn, sig=0, sig_ch=None):
        ch = sig_ch or eng
        sem = self.sem[ch]
        if ch == eng and eng in self.AUTOSYNC:
            prev = self.cnt[eng]
            self.cnt[eng] += 1
            def run(e, fn=fn, sem=sem, prev=prev):
                e.wait_ge(sem, prev)
                fn(e).then_inc(sem, 1)
            self.q[eng].append(run)
            return self.cnt[eng]
        if sig:
            self.cnt[ch] += sig
            def run(e, fn=fn, sem=sem, sig=sig):
                fn(e).then_inc(sem, sig)
            self.q[eng].append(run)
            return self.cnt[ch]
        self.q[eng].append(lambda e, fn=fn: fn(e))
        return None

    def waitt(self, eng, token):
        if token:
            self.wait(eng, token[0], token[1])

    def wait(self, eng, src, ticket):
        if not ticket or ticket <= 0:
            return
        key = (eng, src)
        if self.waited.get(key, -1) >= ticket:
            return
        self.waited[key] = ticket
        sem = self.sem[src]
        self.q[eng].append(lambda e, sem=sem, t=ticket: e.wait_ge(sem, t))

    def barrier(self, skip=()):
        snap = dict(self.cnt)
        srcs = [s for s in self.ENGINES + self.CHANNELS
                if s not in skip and s not in ("sy", "g")]
        for w in self.ENGINES:
            if w in skip:
                continue
            for s in srcs:
                if w != s:
                    self.wait(w, s, snap[s])

    def run(self, block):
        q = self.q
        @block.sync
        def _(e):
            for fn in q["sy"]:
                fn(e)
        @block.tensor
        def _(e):
            for fn in q["te"]:
                fn(e)
        @block.vector
        def _(e):
            for fn in q["v"]:
                fn(e)
        @block.scalar
        def _(e):
            for fn in q["s"]:
                fn(e)
        @block.gpsimd
        def _(e):
            for fn in q["g"]:
                fn(e)


def _build(stage):
    nc = bass.Bass("TRN2", target_bir_lowering=False, debug=False)

    def din(name, shape, dt=F32):
        return nc.dram_tensor(name, shape, dt, kind="ExternalInput").ap()

    x_in = din("x", [T, D])
    wqkv_t = din("wqkv_t", [24, NC, 128, 128])
    wout_t = din("wout_t", [NC, 16, 64, 128])
    w1t = din("w1t", [D, Ff])
    w2t = din("w2t", [Ff, D])
    a_fc = din("a_fc", [RANK, D])
    b_fct = din("b_fct", [RANK, Ff])
    a_pj = din("a_pj", [RANK, Ff])
    b_pjt = din("b_pjt", [RANK, D])
    bias_cols = din("bias_cols", [128, 72])
    rows_f = din("rows_f", [1, 1024])
    rows_b = din("rows_b", [1, 2048])
    cosq = din("cosq", [128, T])
    sinq = din("sinq", [128, T])
    costm = din("costm", [128, 512])
    sintm = din("sintm", [128, 512])
    identf = din("identf", [128, 128])

    y_out = nc.dram_tensor("y", [T, D], F32, kind="ExternalOutput").ap()

    w1eff = nc.dram_tensor("w1eff", [NF * NC, 128, 128], BF16).ap()
    w2eff = nc.dram_tensor("w2eff", [NC * NF, 128, 128], BF16).ap()
    cc_in = nc.dram_tensor("cc_in", [64, NH * 65], F32).ap()
    cc_out = nc.dram_tensor("cc_out", [64, NH * 65], F32).ap()

    dbg_outs = {}
    def dbg(name, shape, dt=F32):
        if name in _DBG:
            dbg_outs[name] = nc.dram_tensor("dbg_" + name, shape, dt,
                                            kind="ExternalOutput").ap()
            return dbg_outs[name]
        return None

    with ExitStack() as stack:
        e_ = stack.enter_context
        p = P(nc)
        p.init_sems(stack)

        xres = e_(nc.sbuf_tensor("xres", [128, NC * T], F32))
        arena = e_(nc.sbuf_tensor("arena", [128, 58 * 1024], BF16))
        cosq_t = e_(nc.sbuf_tensor("cosq_t", [128, T], BF16))
        sinq_t = e_(nc.sbuf_tensor("sinq_t", [128, T], BF16))
        costm_t = e_(nc.sbuf_tensor("costm_t", [128, 512], BF16))
        sintm_t = e_(nc.sbuf_tensor("sintm_t", [128, 512], BF16))
        ident_t = e_(nc.sbuf_tensor("ident_t", [128, 128], F32))
        onesr = e_(nc.sbuf_tensor("onesr", [128, 128], F32R))
        onesb = e_(nc.sbuf_tensor("onesb", [128, 128], BF16))
        bias_t = e_(nc.sbuf_tensor("bias_t", [128, 72], F32))
        statr = e_(nc.sbuf_tensor("statr", [128, 64], F32))
        rowr = e_(nc.sbuf_tensor("rowr", [128, 2048], F32R))
        t1bt = e_(nc.sbuf_tensor("t1bt", [128, 1536], F32))
        psA = e_(nc.psum_tensor("psA", [128, 2048], F32))
        psB = e_(nc.psum_tensor("psB", [128, 2048], F32))

        def bank(i, w=1):
            t_ = psA if i < 4 else psB
            return t_[:, (i % 4) * 512:((i % 4) + w) * 512]

        A = arena

        def af32(off, n, part0=0, parts=128):
            return A[part0:part0 + parts, off:off + 2 * n].bitcast(F32)

        def abf(off, n, part0=0, parts=128):
            return A[part0:part0 + parts, off:off + n]

        def rrow(part, slot, n=512):
            off = 55296 + slot * 1024
            return A[part:part + 1, off:off + 2 * n].bitcast(F32)

        def rrow_r(part, slot, n=512):
            return rowr[part:part + 1, slot * 512:slot * 512 + n]

        def gbar():
            for s_ in ("te", "v", "s"):
                p.wait("g", s_, p.cnt[s_])

        def dma(qn, out, in_, ch=0, fence=True):
            chn = {"sy": ("syA", "syB"), "sy2": ("syC", "syD"),
                   "g": ("gA", "gB")}[qn]
            name = chn[ch % 2]
            qq = "sy" if qn.startswith("sy") else "g"
            if fence:
                p.wait(qq, name, p.cnt[name])
            t = p.op(qq, lambda e, o=out, i=in_: e.dma_start(out=o, in_=i),
                     sig=16, sig_ch=name)
            return (name, t)

        dma("g", cosq_t[:], cosq)
        dma("g", sinq_t[:], sinq)
        dma("g", costm_t[:], costm)
        dma("g", sintm_t[:], sintm)
        t_idl = dma("sy", ident_t[:], identf)
        t_bc = dma("sy", bias_t[:], bias_cols, ch=1)
        p.waitt("v", t_bc)
        p.op("g", lambda e: e.memset(onesb[:], 1.0))
        p.op("g", lambda e: e.memset(statr[:, 7:8], -RMAGIC))
        p.op("g", lambda e: e.memset(statr[:, 48:49], RMS_EPS))
        nc.const_aps.register(F32, RMS_EPS, statr[:, 48:49])
        p.op("g", lambda e: e.memset(statr[:, 49:50], EPS_LIN))
        nc.const_aps.register(F32, EPS_LIN, statr[:, 49:50])
        t_ms = p.op("g", lambda e: e.memset(statr[:, 6:7], 1.0), sig=1)
        p.wait("v", "g", t_ms)
        p.op("v", lambda e: e.tensor_copy(
            onesr[:], statr[:, 6:7].broadcast_to([128, 128])))
        p.op("v", lambda e: e.tensor_scalar(
            statr[:, 8:16], bias_t[:, 64:72], 0.5, None, ALU.mult), sig=1)

        xtm0 = af32(49152, 1024)
        xtm1 = af32(51200, 1024)
        t_ev = {}
        p.waitt("te", t_idl)
        for lc in range(16):
            buf = xtm0 if lc % 2 == 0 else xtm1
            if lc >= 2:
                p.wait("sy", "s", t_ev[(lc - 2) * NC + NC - 1])
            t_ld = dma("sy", buf, x_in[lc * 128:(lc + 1) * 128, :], ch=lc)
            p.waitt("te", t_ld)
            for c in range(NC):
                bk = (lc * NC + c) % 2
                if (lc * NC + c) >= 2:
                    p.wait("te", "s", t_ev[(lc * NC + c) - 2])
                p.op("te", lambda e, bk=bk, buf=buf, c=c: e.transpose(
                    bank(bk)[0:128, 0:128], buf[:, c * 128:(c + 1) * 128],
                    ident_t[:]), sig=1)
                p.wait("s", "te", p.cnt["te"])
                t_ev[lc * NC + c] = p.op("s", lambda e, bk=bk, c=c, lc=lc:
                    e.activation(
                        xres[:, c * T + lc * 128:c * T + (lc + 1) * 128],
                        bank(bk)[0:128, 0:128], AF.Copy), sig=1)

        def emit_wprep(wT, aT, bT, weff, kin, kout, guard=False,
                       t_redx=None, coissue=None):
            nkc = kin // 128
            av = af32(8192, kin, parts=16)
            bv = af32(16384, kout, parts=16)
            t_a = dma("sy", av, aT)
            t_b = dma("sy", bv, bT, ch=1)
            p.waitt("v", t_a)
            t_a2 = p.op("v", lambda e: e.tensor_scalar(
                av, av, 2.0, None, ALU.mult), sig=1)
            if t_redx is not None:
                t_red = t_redx
            else:
                if guard:
                    p.wait("g", "v", p.cnt["v"])
                    p.wait("g", "s", p.cnt["s"])
                pbufs = [af32(0, 4096), af32(24576, 4096),
                         af32(32768, 4096), af32(40960, 4096)]
                chq = [("sy", 0), ("sy", 1), ("g", 0), ("g", 1)]
                t_red = {}
                for i in range(nkc):
                    qn, cc = chq[i % 4]
                    if i >= 4:
                        p.wait(qn, "v", t_red[i - 4])
                    t_pl = dma(qn, pbufs[i % 4][:, 0:kout],
                               wT[i * 128:(i + 1) * 128, :], ch=cc)
                    p.waitt("v", t_pl)
                    t_red[i] = p.op("v",
                        lambda e, b=pbufs[i % 4], i=i, kout=kout:
                        e.reduce_max(
                            statr[:, 16 + i:17 + i], b[:, 0:kout],
                            axis=mybir.AxisListType.X,
                            apply_absolute_value=True), sig=1)
            p.wait("s", "v", t_red[nkc - 1])
            t_fl = p.op("s", lambda e, nkc=nkc: e.copy(
                statr[:, 16:16 + nkc], statr[:, 16:16 + nkc]), sig=1)
            p.wait("v", "s", t_fl)
            t_pm = p.op("v", lambda e, nkc=nkc: e.reduce_max(
                statr[:, 4:5], statr[:, 16:16 + nkc],
                axis=mybir.AxisListType.X), sig=1)
            p.wait("s", "v", t_pm)
            t_fl2 = p.op("s", lambda e: e.copy(statr[:, 4:5], statr[:, 4:5]),
                         sig=1)
            p.wait("te", "s", t_fl2)
            p.op("te", lambda e: e.transpose(
                bank(0)[0:1, 0:128], statr[:, 4:5], ident_t[:]), sig=1)
            p.wait("v", "te", p.cnt["te"])
            t_gm = p.op("v", lambda e: e.reduce_max(
                rrow(0, 0, 1), bank(0)[0:1, 0:128],
                axis=mybir.AxisListType.X), sig=1)
            p.wait("s", "v", t_gm)
            p.op("s", lambda e: e.mul(
                rrow(0, 1, 1), rrow(0, 0, 1), 1.0 / 127.0))
            p.op("s", lambda e: e.activation(
                rrow(0, 2, 1), rrow(0, 1, 1), AF.Ln))
            t_fl3 = p.op("s", lambda e: e.activation(
                rrow(0, 3, 1), rrow(0, 2, 1), AF.Exp, scale=-1.0), sig=1)
            p.wait("te", "s", t_fl3)
            p.op("te", lambda e: e.matmul(
                bank(0)[:, 0:1], onesr[0:1, :].bitcast(F32), rrow(0, 1, 1),
                start=True, stop=True))
            p.op("te", lambda e: e.matmul(
                bank(1)[:, 0:1], onesr[0:1, :].bitcast(F32), rrow(0, 3, 1),
                start=True, stop=True), sig=1)
            p.wait("s", "te", p.cnt["te"])
            p.op("s", lambda e: e.copy(statr[:, 2:3], bank(0)[:, 0:1]))
            t_stc = p.op("s", lambda e: e.copy(statr[:, 3:4], bank(1)[:, 0:1]),
                         sig=1)
            last01 = max(i for i in range(nkc) if i % 4 < 2)
            p.wait("sy", "v", t_red[last01])
            p.wait("s", "v", t_red[last01])
            p.wait("v", "s", t_stc)
            p.wait("te", "v", t_a2)
            p.waitt("te", t_b)

            pb = [af32(0, 512), af32(1024, 512), af32(26624, 512)]
            pq = [af32(2048, 512), af32(5120, 512), af32(28672, 512)]
            pqm = [af32(3072, 512), af32(6144, 512), af32(29696, 512)]
            wb = [abf(4096, 512), abf(4608, 512)]
            t_ld, t_ts, t_q, t_qm, t_st, t_lor = {}, {}, {}, {}, {}, {}
            nob = kout // 512
            for ob in range(nob):
                for i in range(nkc):
                    it = ob * nkc + i
                    if it >= 3:
                        p.wait("sy", "v", t_ts[it - 3])
                    t_ld[it] = dma("sy", pb[it % 3],
                                   wT[i * 128:(i + 1) * 128,
                                      ob * 512:(ob + 1) * 512], ch=it)
                    bkk = 2 + it % 2
                    if it >= 2:
                        p.wait("te", "v", t_q[it - 2])
                    t_lor[it] = p.op("te", lambda e, bkk=bkk, i=i, ob=ob:
                        e.matmul(bank(bkk),
                                 av[:, i * 128:(i + 1) * 128].bitcast(F32),
                                 bv[:, ob * 512:(ob + 1) * 512].bitcast(F32),
                                 start=True, stop=True), sig=1)
                    p.waitt("v", t_ld[it])
                    if it >= 3:
                        p.wait("v", "s", t_qm[it - 3])
                    t_ts[it] = p.op("v", lambda e, it=it: e.tensor_scalar(
                        pq[it % 3], pb[it % 3], statr[:, 3:4], RMAGIC,
                        ALU.mult, ALU.add), sig=1)
                    p.wait("s", "v", t_ts[it])
                    if it >= 3:
                        p.wait("s", "v", t_q[it - 3])
                    t_qm[it] = p.op("s", lambda e, it=it: e.activation(
                        pqm[it % 3], pq[it % 3], AF.Identity,
                        bias=statr[:, 7:8]), sig=1)
                    p.wait("v", "s", t_qm[it])
                    p.wait("v", "te", t_lor[it])
                    if it >= 2:
                        p.waitt("v", t_st[it - 2])
                    t_q[it] = p.op("v", lambda e, it=it, bkk=bkk:
                        e.scalar_tensor_tensor(
                            wb[it % 2], pqm[it % 3], statr[:, 2:3], bank(bkk),
                            ALU.mult, ALU.add), sig=1)
                    p.wait("g", "v", t_q[it])
                    for mm4 in range(4):
                        m = ob * 4 + mm4
                        t_st[it] = dma("g", weff[m * nkc + i],
                                       wb[it % 2][:, mm4 * 128:(mm4 + 1) * 128],
                                       ch=it, fence=(mm4 == 0))
                    if coissue is not None and it % 2 == 1:
                        coissue(it // 2)
            p.barrier(skip=("g",))

        if stage >= 1:
            co_tred = {}
            pbufs2 = [af32(32768, 1024), af32(34816, 1024)]

            def co_piece(jj):
                if jj == 0:
                    p.wait("sy", "v", p.cnt["v"])
                if jj >= 2:
                    p.wait("sy", "v", co_tred[jj - 2])
                t_pl = dma("sy2", pbufs2[jj % 2],
                           w2t[jj * 128:(jj + 1) * 128, :], ch=jj)
                p.waitt("v", t_pl)
                co_tred[jj] = p.op("v", lambda e, b=pbufs2[jj % 2], jj=jj:
                    e.reduce_max(statr[:, 16 + jj:17 + jj], b,
                                 axis=mybir.AxisListType.X,
                                 apply_absolute_value=True), sig=1)

            emit_wprep(w1t, a_fc, b_fct, w1eff, D, Ff, coissue=co_piece)
            dr = dbg("rows1", [1, 2048])
            if dr is not None:
                dma("sy", dr, A[0:1, 55296:59392].bitcast(F32))
                p.barrier(skip=("g",))
            d0 = dbg("statr1", [128, 64])
            if d0 is not None:
                dma("sy", d0, statr[:])
                p.barrier(skip=("g",))
            emit_wprep(w2t, a_pj, b_pjt, w2eff, Ff, D, t_redx=co_tred)
            d = dbg("w1eff0", [8, 128, 128], BF16)
            if d is not None:
                dma("sy", d, w1eff[0:8])
                p.barrier(skip=("g",))

        t1b = [t1bt[:, 0:512], t1bt[:, 512:1024], t1bt[:, 1024:1536]]

        def emit_ln(nblocks, lnb_slice, bs1, bs2, bab, bcb):
            xb = [abf(51200 + i * 512, 512) for i in range(4)]
            x2b = [abf(53248 + i * 512, 512) for i in range(4)]
            t_fin_prev = {0: 0, 1: 0}
            t_ap_prev = {0: 0, 1: 0}
            t_bc_prev = {0: 0, 1: 0}
            t_f4_prev = {0: 0, 1: 0}
            prev_s1mm = {0: {}, 1: {}}
            prev_s2mm = {0: {}, 1: {}}
            for inb, nb in enumerate(nblocks):
                par = inb % 2
                rp = 32 * par
                b1, b2, ba, bc = (bs1, bs2, bab, bcb) if par == 0 else \
                    (bs1 - 4, bs2 - 4, bab - 4, bcb - 4)
                onesr_p = onesr[rp:rp + 1, :]
                t_cast, t_sq, t_s1mm, t_s2mm = {}, {}, {}, {}
                p.wait("te", "v", t_fin_prev[par])
                for c in range(NC):
                    xs = xres[:, c * T + nb * 512:c * T + (nb + 1) * 512]
                    bi = 2 * par + (c % 2)
                    if c >= 2:
                        p.wait("v", "te", t_s1mm[c - 2])
                        p.wait("s", "te", t_s2mm[c - 2])
                    elif prev_s1mm[par]:
                        p.wait("v", "te", prev_s1mm[par][NC - 2 + c])
                        p.wait("s", "te", prev_s2mm[par][NC - 2 + c])
                    t_cast[c] = p.op("v", lambda e, xs=xs, bi=bi: e.tensor_copy(
                        xb[bi], xs), sig=1)
                    t_sq[c] = p.op("s", lambda e, xs=xs, bi=bi: e.activation(
                        x2b[bi], xs, AF.Square), sig=1)
                    p.wait("te", "v", t_cast[c])
                    p.wait("te", "s", t_sq[c])
                    t_s1mm[c] = p.op("te", lambda e, c=c, bi=bi, b1=b1: e.matmul(
                        bank(b1)[0:1, :], onesb[:, 0:1], xb[bi],
                        start=(c == 0), stop=(c == NC - 1)), sig=1)
                    t_s2mm[c] = p.op("te", lambda e, c=c, bi=bi, b2=b2: e.matmul(
                        bank(b2)[0:1, :], onesb[:, 0:1], x2b[bi],
                        start=(c == 0), stop=(c == NC - 1)), sig=1)
                p.wait("v", "te", t_s2mm[NC - 1])
                p.wait("v", "s", t_f4_prev[par])
                p.op("v", lambda e, rp=rp, b1=b1: e.tensor_scalar(
                    rrow(rp, 0), bank(b1)[0:1, :], 1.0 / D, None, ALU.mult))
                t_f1 = p.op("v", lambda e, rp=rp, b2=b2: e.tensor_scalar(
                    rrow(rp, 3), bank(b2)[0:1, :], 1.0 / D, LN_EPS, ALU.mult,
                    ALU.add), sig=1)
                t_fin_prev[par] = t_f1
                p.wait("s", "v", t_f1)
                t_f2 = p.op("s", lambda e, rp=rp: e.activation(
                    rrow(rp, 1), rrow(rp, 0), AF.Square), sig=1)
                p.wait("v", "s", t_f2)
                t_f3 = p.op("v", lambda e, rp=rp: e.tensor_sub(
                    rrow(rp, 3), rrow(rp, 3), rrow(rp, 1)), sig=1)
                p.wait("s", "v", t_f3)
                t_ln = p.op("s", lambda e, rp=rp: e.activation(
                    rrow(rp, 3), rrow(rp, 3), AF.Ln), sig=1)
                p.wait("v", "s", t_ln)
                t_cpv = p.op("v", lambda e, rp=rp: e.tensor_copy(
                    rrow(rp, 1), rrow(rp, 3)), sig=1)
                p.wait("s", "v", t_cpv)
                p.wait("s", "te", t_bc_prev[par])
                t_f4 = p.op("s", lambda e, rp=rp: e.activation(
                    rrow_r(rp, 1), rrow(rp, 1), AF.Exp, scale=-0.5), sig=1)
                t_f4_prev[par] = t_f4
                p.wait("v", "s", t_f4)
                p.wait("v", "te", t_bc_prev[par])
                t_f5 = p.op("v", lambda e, rp=rp: e.scalar_tensor_tensor(
                    rrow_r(rp, 2), rrow(rp, 0), -1.0,
                    rrow_r(rp, 1).bitcast(F32), ALU.mult, ALU.mult), sig=1)
                p.wait("te", "v", t_f5)
                p.wait("te", "v", t_ap_prev[par])
                p.op("te", lambda e, rp=rp, ba=ba, o_=onesr_p: e.matmul(
                    bank(ba), o_, rrow_r(rp, 1), start=True, stop=True))
                t_bc2 = p.op("te", lambda e, rp=rp, bc=bc, o_=onesr_p: e.matmul(
                    bank(bc), o_, rrow_r(rp, 2), start=True,
                    stop=True), sig=1)
                t_bc_prev[par] = t_bc2
                p.wait("v", "te", t_bc2)
                for c in range(NC):
                    xs = xres[:, c * T + nb * 512:c * T + (nb + 1) * 512]
                    p.op("v", lambda e, xs=xs, c=c, ba=ba: e.tensor_mul(
                        t1b[c % 3], xs, bank(ba)))
                    if c >= 1:
                        p.op("v", lambda e, c=c, nb=nb, bc=bc: e.tensor_add(
                            lnb_slice(c - 1, nb), t1b[(c - 1) % 3], bank(bc)))
                t_ap_prev[par] = p.op("v", lambda e, nb=nb, bc=bc: e.tensor_add(
                    lnb_slice(NC - 1, nb), t1b[(NC - 1) % 3], bank(bc)), sig=1)
                prev_s1mm[par], prev_s2mm[par] = t_s1mm, t_s2mm

        def emit_ffn():
            for H in (0, 1):
                def lnb_slice(c, nb, H=H):
                    return abf(32768 + c * 1024 + (nb - 2 * H) * 512, 512)
                emit_ln([2 * H, 2 * H + 1], lnb_slice, 4, 5, 6, 7)
                p.barrier(skip=("g",))
                if H == 0:
                    d = dbg("ln1b", [128, 8192], BF16)
                    if d is not None:
                        dma("sy", d, abf(32768, 8192))
                        p.barrier(skip=("g",))

                w1b = [abf(49152, 1024), abf(50176, 1024)]
                t_fcmm, t_gel = {}, {}
                for m in range(NF):
                    if m >= 2:
                        p.wait("sy", "te", t_fcmm[m - 2])
                    t_wld = dma("sy",
                                w1b[m % 2].rearrange("p (k c) -> p k c", k=8),
                                w1eff[m * 8:(m + 1) * 8]
                                .rearrange("k p c -> p k c"), ch=m)
                    p.waitt("te", t_wld)
                    if m >= 2:
                        p.wait("te", "s", t_gel[m - 2])
                    bk0 = 0 if m % 2 == 0 else 2
                    for k in range(NC):
                        for nb in range(2):
                            p.op("te", lambda e, bk0=bk0, k=k, nb=nb, m=m:
                                 e.matmul(bank(bk0 + nb),
                                          w1b[m % 2][:, k * 128:(k + 1) * 128],
                                          abf(32768 + k * 1024 + nb * 512, 512),
                                          start=(k == 0), stop=(k == NC - 1)),
                                 sig=(1 if (k == NC - 1 and nb == 1) else 0))
                    t_fcmm[m] = p.cnt["te"]
                    p.wait("s", "te", t_fcmm[m])
                    t_gel[m] = p.op("s", lambda e, bk0=bk0, m=m: e.activation(
                        abf(m * 1024, 1024), bank(bk0, 2), AF.Gelu,
                        bias=bias_t[:, 24 + m:25 + m]), sig=1)

                if H == 0:
                    d = dbg("hblk", [128, 8192], BF16)
                    if d is not None:
                        p.barrier(skip=("g",))
                        dma("sy", d, abf(0, 8192))
                        p.barrier(skip=("g",))
                w2b = [abf(40960, 4096), abf(45056, 4096)]
                t_pjmm, t_pj = {}, {}
                for m in range(NC):
                    if m >= 2:
                        p.wait("sy", "te", t_pjmm[m - 2])
                    t_w2 = dma("sy",
                               w2b[m % 2].rearrange("p (k c) -> p k c", k=32),
                               w2eff[m * 32:(m + 1) * 32]
                               .rearrange("k p c -> p k c"), ch=m)
                    p.waitt("te", t_w2)
                    if m >= 2:
                        p.wait("te", "v", t_pj[m - 2])
                    bk0 = 0 if m % 2 == 0 else 2
                    if m < 2:
                        p.wait("te", "s", t_gel[NF - 2 + m])
                    for k in range(NF):
                        p.wait("te", "s", t_gel[k])
                        for nb in range(2):
                            p.op("te", lambda e, bk0=bk0, k=k, nb=nb, m=m:
                                 e.matmul(bank(bk0 + nb),
                                          w2b[m % 2][:, k * 128:(k + 1) * 128],
                                          abf(k * 1024 + nb * 512, 512),
                                          start=(k == 0), stop=(k == NF - 1)),
                                 sig=(1 if (k == NF - 1 and nb == 1) else 0))
                    t_pjmm[m] = p.cnt["te"]
                    p.wait("v", "te", t_pjmm[m])
                    xs = xres[:, m * T + H * 1024:m * T + (H + 1) * 1024]
                    t_pj[m] = p.op("v", lambda e, bk0=bk0, xs=xs:
                        e.scalar_tensor_tensor(
                            xs, bank(bk0, 2), 0.5, xs, ALU.mult, ALU.add), sig=1)
                for m in range(NC):
                    p.wait("s", "v", t_pj[m])
                    xs = xres[:, m * T + H * 1024:m * T + (H + 1) * 1024]
                    p.op("s", lambda e, xs=xs, m=m: e.activation(
                        xs, xs, AF.Identity, bias=statr[:, 8 + m:9 + m]),
                        sig=(1 if m == NC - 1 else 0))
                p.barrier(skip=("g",))

        if stage >= 1:
            emit_ffn()

        def emit_attn():
            gbar()
            gkbc = af32(16384, 1024)
            bkbc = abf(18432, 1024)
            bvbc = abf(19456, 1024)
            rowbk = A[0:1, 20480:21504]
            rowbv = A[0:1, 21504:22528]
            vkacc = af32(22528, 1040, parts=64)
            t_rf = dma("sy", A[0:1, 55296:55296 + 2048].bitcast(F32), rows_f)
            t_rb = dma("g", A[0:1, 20480:22528], rows_b)
            p.waitt("v", t_rf)
            t_rr = p.op("v", lambda e: e.tensor_copy(
                rowr[0:1, 0:1024],
                A[0:1, 55296:55296 + 2048].bitcast(F32)), sig=1)
            p.wait("te", "v", t_rr)
            p.waitt("te", t_rb)
            for nb in range(2):
                p.op("te", lambda e, nb=nb: e.matmul(
                    bank(0 + nb), onesr[0:1, :],
                    rowr[0:1, nb * 512:(nb + 1) * 512],
                    start=True, stop=True))
                p.op("te", lambda e, nb=nb: e.matmul(
                    bank(2 + nb), onesb[0:1, :],
                    rowbk[0:1, nb * 512:(nb + 1) * 512], start=True, stop=True))
                p.op("te", lambda e, nb=nb: e.matmul(
                    bank(4 + nb), onesb[0:1, :],
                    rowbv[0:1, nb * 512:(nb + 1) * 512], start=True,
                    stop=True), sig=(1 if nb == 1 else 0))
            t_bcmm = p.cnt["te"]
            p.wait("v", "te", t_bcmm)
            p.op("v", lambda e: e.tensor_copy(gkbc, psA[:, 0:1024]))
            p.op("v", lambda e: e.tensor_copy(bkbc, psA[:, 1024:2048]))
            p.op("v", lambda e: e.tensor_copy(bvbc, psB[:, 0:1024]))
            p.op("v", lambda e: e.memset(vkacc, 0.0), sig=1)
            p.barrier(skip=("g",))

            def ln2_slice(c, nb):
                return abf(c * 2048 + nb * 512, 512)
            emit_ln([0, 1, 2, 3], ln2_slice, 4, 5, 6, 7)
            p.barrier(skip=("g",))
            d = dbg("ln2b", [128, 16384], BF16)
            if d is not None:
                dma("sy", d, abf(0, 16384))
                p.barrier(skip=("g",))

            wkvA = abf(32768, 4096)
            wkvB = abf(49152, 4096)
            k_sb = abf(36864, 2048)
            kf = [abf(38912 + i * 1024, 1024) for i in range(4)]
            ktl = abf(43008, 1024)
            u1 = abf(44032, 512)
            u2 = abf(44544, 512)
            vp = [abf(45056 + i * 1024, 1024) for i in range(4)]
            t_kvld, t_kvmm, t_kvev, t_rope, t_vkadd = {}, {}, {}, {}, {}
            t_vkmm2, t_k2d = {}, {}
            gbar()

            def prev_evkey(g, bb2, lq):
                if bb2 > 0:
                    return (g, bb2 - 1, lq)
                if g > 0:
                    return (g - 1, 3, lq)
                return None

            def emit_vk(g):
                for lq in range(2):
                    lc = 2 * g + lq
                    kfb = kf[lc % 4]
                    p.wait("te", "s", t_rope[lc])
                    p.wait("te", "v", t_kvev[(g, 3, lq)])
                    p.wait("te", "v", t_vkadd.get(lc - 1, 0))
                    for h in range(NH):
                        colv = [512 + 65 * h, 1024 + 65 * (h - 7),
                                1536 + 65 * (h - 14)][h // 7]
                        vks = psB[0:64, colv:colv + 65]
                        p.op("te", lambda e, h=h, kfb=kfb, lc=lc, vks=vks:
                             e.matmul(vks[:, 0:64],
                                      kfb[:, 64 * h:64 * h + 64],
                                      vp[lc % 4][:, 64 * h:64 * h + 64],
                                      start=True, stop=True), sig=1)
                        p.op("te", lambda e, h=h, kfb=kfb, vks=vks:
                             e.matmul(vks[:, 64:65],
                                      kfb[:, 64 * h:64 * h + 64],
                                      onesb[:, 0:1],
                                      start=True, stop=True), sig=1)
                    t_vkh = p.cnt["te"]
                    t_vkmm2[(g, lq)] = t_vkh
                    p.wait("v", "te", t_vkh)
                    p.op("v", lambda e: e.tensor_add(
                        vkacc[:, 0:455], vkacc[:, 0:455],
                        psB[0:64, 512:512 + 455]))
                    p.op("v", lambda e: e.tensor_add(
                        vkacc[:, 455:910], vkacc[:, 455:910],
                        psB[0:64, 1024:1024 + 455]))
                    t_vkadd[lc] = p.op("v", lambda e: e.tensor_add(
                        vkacc[:, 910:1040], vkacc[:, 910:1040],
                        psB[0:64, 1536:1536 + 130]), sig=1)

            for g in range(8):
                for bb2 in range(4):
                    it = g * 4 + bb2
                    wb = wkvA if it % 2 == 0 else wkvB
                    m0 = 8 + 4 * bb2
                    if it >= 2:
                        p.wait("g", "te", t_kvmm[it - 2])
                    tl = None
                    for kk in range(NC):
                        tl = dma("g",
                                 wb[:, kk * 512:(kk + 1) * 512]
                                 .rearrange("p (a c) -> p a c", a=4),
                                 wqkv_t[m0:m0 + 4, kk]
                                 .rearrange("a p c -> p a c"), ch=it,
                                 fence=(kk == 0))
                    t_kvld[it] = tl
                    p.waitt("te", t_kvld[it])
                    for kk in range(NC):
                        for lq in range(2):
                            lc = 2 * g + lq
                            if kk == 0:
                                pk = prev_evkey(g, bb2, lq)
                                if pk is not None:
                                    p.wait("te", "v", t_kvev.get(pk, 0))
                            p.op("te", lambda e, kk=kk, lq=lq, lc=lc, wb=wb:
                                 e.matmul(bank(lq),
                                          A[:, kk * 2048 + lc * 128:
                                            kk * 2048 + lc * 128 + 128],
                                          wb[:, kk * 512:(kk + 1) * 512],
                                          start=(kk == 0), stop=(kk == NC - 1)),
                                 sig=(1 if kk == NC - 1 else 0))
                    t_kvmm[it] = p.cnt["te"]
                    p.wait("v", "te", t_kvmm[it])
                    if bb2 < 2:
                        for lq in range(2):
                            if g >= 1 and bb2 == 0:
                                p.wait("v", "s", t_k2d.get((g - 1, lq), 0))
                            t_kvev[(g, bb2, lq)] = p.op("v",
                                lambda e, lq=lq, bb2=bb2: e.tensor_tensor(
                                    k_sb[:, lq * 1024 + bb2 * 512:
                                         lq * 1024 + bb2 * 512 + 512],
                                    bank(lq),
                                    bkbc[:, bb2 * 512:(bb2 + 1) * 512],
                                    ALU.add), sig=1)
                    if bb2 == 1:
                        for lq in range(2):
                            lc = 2 * g + lq
                            kfb = kf[lc % 4]
                            p.wait("s", "v", t_kvev[(g, 1, lq)])
                            if g >= 2:
                                p.wait("s", "te", t_vkmm2.get((g - 2, lq), 0))
                            t_k2 = p.op("s", lambda e, lq=lq, kfb=kfb:
                                e.activation(kfb,
                                             k_sb[:, lq * 1024:(lq + 1) * 1024],
                                             AF.Square,
                                             accum_out=statr[:, 0:1]), sig=1)
                            t_k2d[(g, lq)] = t_k2
                            p.op("s", lambda e: e.activation(
                                statr[:, 1:2], statr[:, 0:1], AF.Ln,
                                bias=RMS_EPS, scale=1.0 / D))
                            t_i2 = p.op("s", lambda e: e.activation(
                                statr[:, 1:2], statr[:, 1:2], AF.Exp,
                                scale=-0.5), sig=1)
                            p.wait("v", "s", t_i2)
                            p.op("v", lambda e, lq=lq: e.scalar_tensor_tensor(
                                ktl, k_sb[:, lq * 1024:(lq + 1) * 1024],
                                statr[:, 1:2], gkbc, ALU.mult, ALU.mult))
                            rp = lambda t_: t_.rearrange(
                                "p (h d) -> p h d", d=64)[:, :, 0:32]
                            ip = lambda t_: t_.rearrange(
                                "p (h d) -> p h d", d=64)[:, :, 32:64]
                            cpat = costm_t[:, lc * 32:(lc + 1) * 32] \
                                .unsqueeze(1).broadcast_to([128, 16, 32])
                            spat = sintm_t[:, lc * 32:(lc + 1) * 32] \
                                .unsqueeze(1).broadcast_to([128, 16, 32])
                            u1p = u1.rearrange("p (h d) -> p h d", d=32)
                            u2p = u2.rearrange("p (h d) -> p h d", d=32)
                            p.op("v", lambda e, cpat=cpat: e.tensor_tensor(
                                u1p, rp(ktl), cpat, ALU.mult))
                            p.op("v", lambda e, spat=spat: e.tensor_tensor(
                                u2p, rp(ktl), spat, ALU.mult))
                            p.op("v", lambda e, spat=spat, kfb=kfb:
                                 e.tensor_tensor(rp(kfb), ip(ktl), spat,
                                                 ALU.mult))
                            p.op("v", lambda e, kfb=kfb: e.tensor_tensor(
                                rp(kfb), u1p, rp(kfb), ALU.subtract))
                            p.op("v", lambda e, cpat=cpat, kfb=kfb:
                                 e.tensor_tensor(ip(kfb), ip(ktl), cpat,
                                                 ALU.mult))
                            t_rp6 = p.op("v", lambda e, kfb=kfb:
                                e.tensor_tensor(ip(kfb), u2p, ip(kfb),
                                                ALU.add), sig=1)
                            p.wait("s", "v", t_rp6)
                            t_rope[lc] = p.op("s", lambda e, kfb=kfb:
                                e.activation(kfb, kfb, AF.Relu), sig=1)
                    if bb2 >= 2:
                        for lq in range(2):
                            lc = 2 * g + lq
                            if g >= 2 and bb2 == 2:
                                p.wait("v", "te", t_vkmm2.get((g - 2, lq), 0))
                            t_kvev[(g, bb2, lq)] = p.op("v",
                                lambda e, lc=lc, bb2=bb2, lq=lq: e.tensor_tensor(
                                    vp[lc % 4][:, (bb2 - 2) * 512:(bb2 - 1) * 512],
                                    bank(lq),
                                    bvbc[:, (bb2 - 2) * 512:(bb2 - 1) * 512],
                                    ALU.add), sig=1)
                if g >= 1:
                    emit_vk(g - 1)
            emit_vk(7)
            t_a2_te = p.cnt["te"]
            d = dbg("kf15", [128, 1024], BF16)
            if d is not None:
                p.barrier(skip=("g",))
                dma("sy", d, kf[1])
                d2 = dbg("vp15", [128, 1024], BF16)
                dma("sy", d2, vp[1], ch=1)
                d3 = dbg("ksb15", [128, 1024], BF16)
                dma("sy", d3, k_sb[:, 1024:2048])
                p.barrier(skip=("g",))

            p.wait("sy", "v", t_vkadd[T // 128 - 1])
            d = dbg("vkacc", [64, 1040])
            if d is not None:
                dma("sy", d, vkacc)
            t_cci = dma("sy", cc_in, vkacc)
            p.waitt("g", t_cci)
            t_cc = p.op("g", lambda e: e.collective_compute(
                "AllReduce", ALU.add,
                replica_groups=[[0, 1], [2, 3], [4, 5], [6, 7]],
                ins=[cc_in.opt()], outs=[cc_out.opt()]), sig=1)
            vkrep = af32(53248, 520)
            vkrepb = abf(54288, 520)
            t_sc2 = {}
            for a in range(4):
                inr = cc_out[0:32, :].rearrange(
                    "p (j x) -> p j x", x=260)[:, :, 65 * a:65 * a + 65]
                ini = cc_out[32:64, :].rearrange(
                    "p (j x) -> p j x", x=260)[:, :, 65 * a:65 * a + 65]
                outr = vkrep[32 * a:32 * a + 32, 0:260].rearrange(
                    "p (j x) -> p j x", x=65)
                outi = vkrep[32 * a:32 * a + 32, 260:520].rearrange(
                    "p (j x) -> p j x", x=65)
                p.wait("g", "g", t_cc)
                dma("g", outr, inr, ch=1, fence=(a == 0))
                t_sc2[a] = dma("g", outi, ini, ch=1, fence=False)
            d = dbg("vkrep", [128, 520])
            if d is not None:
                p.waitt("sy", t_sc2[3])
                dma("sy", d, vkrep)

            wqb = [abf(32768, 1024), abf(33792, 1024)]
            q2b = [abf(34816, 1024), abf(35840, 1024)]
            wqs = [af32(36864, 1024), af32(38912, 1024)]
            t_qmm, t_qev, t_qst, t_cast = {}, {}, {}, {}
            p.waitt("s", t_cci)
            p.wait("sy", "te", t_a2_te)
            p.wait("sy", "s", t_k2d[(7, 1)])
            p.wait("sy", "v", p.cnt["v"])
            p.wait("v", "te", t_a2_te)
            for m in range(NC):
                if m >= 2:
                    p.wait("sy", "v", t_cast[m - 2])
                t_wq = dma("sy", wqs[m % 2].rearrange("p (k c) -> p k c", k=8),
                           wqkv_t[m].rearrange("k p c -> p k c"), ch=m)
                p.waitt("v", t_wq)
                if m >= 2:
                    p.wait("v", "te", t_qmm[(m - 2, 1)])
                t_cast[m] = p.op("v", lambda e, m=m: e.tensor_copy(
                    wqb[m % 2], wqs[m % 2]), sig=1)
                p.wait("te", "v", t_cast[m])
                for hf in range(2):
                    bk0 = 0 if (2 * m + hf) % 2 == 0 else 2
                    if m >= 1:
                        p.wait("te", "s", t_qev.get((m - 1, hf), 0))
                    for kk in range(NC):
                        for nb2 in range(2):
                            p.op("te", lambda e, bk0=bk0, kk=kk, nb2=nb2,
                                 hf=hf, m=m: e.matmul(
                                     bank(bk0 + nb2),
                                     wqb[m % 2][:, kk * 128:(kk + 1) * 128],
                                     abf(kk * 2048 + hf * 1024 + nb2 * 512, 512),
                                     start=(kk == 0), stop=(kk == NC - 1)),
                                 sig=(1 if kk == NC - 1 and nb2 == 1 else 0))
                    t_qmm[(m, hf)] = p.cnt["te"]
                    p.wait("s", "te", t_qmm[(m, hf)])
                    if m >= 1:
                        p.wait("s", "te", t_qst.get((m - 1, hf), 0))
                    p.op("s", lambda e, bk0=bk0, m=m, hf=hf: e.activation(
                        abf(16384 + m * 2048 + hf * 1024, 1024), bank(bk0, 2),
                        AF.Identity, bias=bias_t[:, m:m + 1]))
                    t_qev[(m, hf)] = p.op("s", lambda e, bk0=bk0, m=m, hf=hf:
                        e.activation(q2b[hf], bank(bk0, 2), AF.Square,
                                     bias=bias_t[:, m:m + 1]), sig=1)
                    p.wait("te", "s", t_qev[(m, hf)])
                    for nb2 in range(2):
                        p.op("te", lambda e, hf=hf, nb2=nb2, m=m: e.matmul(
                            bank(4 + 2 * hf + nb2)[0:1, :], onesb[:, 0:1],
                            q2b[hf][:, nb2 * 512:(nb2 + 1) * 512],
                            start=(m == 0), stop=(m == NC - 1)),
                            sig=(1 if nb2 == 1 else 0))
                    t_qst[(m, hf)] = p.cnt["te"]
            t_qstat = p.cnt["te"]
            p.waitt("v", t_sc2[3])
            t_vkb = p.op("v", lambda e: e.tensor_copy(vkrepb, vkrep), sig=1)

            p.wait("s", "te", t_qstat)
            p.op("s", lambda e: e.activation(
                rrow(32, 0, 2048), bank(4, 4)[0:1, :], AF.Ln,
                bias=RMS_EPS, scale=1.0 / D))
            t_inv3 = p.op("s", lambda e: e.activation(
                rowr[0:1, 0:2048], rrow(32, 0, 2048), AF.Exp,
                scale=-0.5), sig=1)

            scr1 = abf(32768, 2048)
            scr2 = abf(34816, 2048)
            scr3 = abf(36864, 2048)
            t_qrope = {}
            p.wait("v", "s", t_qev[(NC - 1, 1)])
            p.wait("v", "te", t_qstat)
            for c in range(4):
                qr = abf(16384 + c * 2048, 2048)
                qi = abf(16384 + (4 + c) * 2048, 2048)
                p.op("v", lambda e, qr=qr: e.tensor_tensor(
                    scr1, qr, cosq_t[:], ALU.mult))
                p.op("v", lambda e, qr=qr: e.tensor_tensor(
                    scr2, qr, sinq_t[:], ALU.mult))
                p.op("v", lambda e, qi=qi: e.tensor_tensor(
                    scr3, qi, sinq_t[:], ALU.mult))
                p.op("v", lambda e, qr=qr: e.tensor_tensor(
                    qr, scr1, scr3, ALU.subtract))
                p.op("v", lambda e, qi=qi: e.tensor_tensor(
                    scr3, qi, cosq_t[:], ALU.mult))
                t_r6 = p.op("v", lambda e, qi=qi: e.tensor_tensor(
                    qi, scr2, scr3, ALU.add), sig=1)
                p.wait("s", "v", t_r6)
                p.op("s", lambda e, qr=qr: e.activation(qr, qr, AF.Relu))
                t_qrope[c] = p.op("s", lambda e, qi=qi: e.activation(
                    qi, qi, AF.Relu), sig=1)
            p.wait("te", "s", t_inv3)
            p.wait("te", "s", t_qrope[3])
            for nb in range(4):
                p.op("te", lambda e, nb=nb: e.matmul(
                    bank(nb), onesr[0:1, :], rrow_r(0, nb),
                    start=True, stop=True), sig=(1 if nb == 3 else 0))
            t_invb = p.cnt["te"]
            p.wait("v", "te", t_invb)
            t_qsc = 0
            p.waitt("v", t_cci)
            for cc2 in range(8):
                for nb in range(4):
                    qs = abf(16384 + cc2 * 2048 + nb * 512, 512)
                    t_qsc = p.op("v", lambda e, qs=qs, nb=nb, cc2=cc2:
                        e.scalar_tensor_tensor(
                            qs, qs, bias_t[:, 8 + cc2:9 + cc2], bank(nb),
                            ALU.mult, ALU.mult),
                        sig=(1 if (cc2 == 7 and nb == 3) else 0))
            p.barrier(skip=("g",))
            d = dbg("qf", [128, 16384], BF16)
            if d is not None:
                dma("sy", d, abf(16384, 16384))
                p.barrier(skip=("g",))

            def attn_off(h):
                return h * 2048 if h < 8 else 32768 + (h - 8) * 2048

            t_evac, t_rcp, t_rbmm, t_div = {}, {}, {}, {}

            def emit_rb(i2):
                rbb = bank((4, 6)[i2 % 2], 2)[0:64, :]
                p.wait("te", "s", t_rcp[i2])
                if i2 >= 2:
                    p.wait("te", "v", t_div[i2 - 2])
                for nb2 in range(2):
                    p.op("te", lambda e, rbb=rbb, i2=i2, nb2=nb2: e.matmul(
                        rbb[:, nb2 * 512:(nb2 + 1) * 512],
                        onesr[64:65, 0:64],
                        rowr[64:65, (i2 % 2) * 1024 + nb2 * 512:
                             (i2 % 2) * 1024 + nb2 * 512 + 512],
                        start=True, stop=True, tile_position=(64, 0)),
                        sig=(1 if nb2 == 1 else 0))
                t_rbmm[i2] = p.cnt["te"]
                p.wait("v", "te", t_rbmm[i2])
                p.wait("v", "s", t_evac[i2])
                aoff = attn_off(i2 // 2) + (i2 % 2) * 1024
                t_div[i2] = p.op("v", lambda e, rbb=rbb, aoff=aoff:
                    e.tensor_tensor(
                        A[0:64, aoff:aoff + 1024],
                        A[0:64, aoff:aoff + 1024], rbb, ALU.mult), sig=1)

            p.wait("te", "v", t_vkb)
            for h in range(NH):
                a = h % 4
                j = h // 4
                tp = (96, 0) if a == 3 else None
                lreal = vkrepb[32 * a:32 * a + 32, 65 * j:65 * j + 65]
                limag = vkrepb[32 * a:32 * a + 32,
                               260 + 65 * j:260 + 65 * j + 65]
                for hf in range(2):
                    i2 = 2 * h + hf
                    bres = 0 if i2 % 2 == 0 else 2
                    res = bank(bres, 2)[0:65, :]
                    if i2 >= 2:
                        p.wait("te", "s", t_evac[i2 - 2])
                    for nb2 in range(2):
                        qoff = 16384 + (h // 4) * 2048 + hf * 1024 + nb2 * 512
                        qr = A[32 * a:32 * a + 32, qoff:qoff + 512]
                        qi = A[32 * a:32 * a + 32,
                               qoff + 8192:qoff + 8192 + 512]
                        p.op("te", lambda e, res=res, lreal=lreal, qr=qr,
                             tp=tp, nb2=nb2: e.matmul(
                                 res[:, nb2 * 512:(nb2 + 1) * 512], lreal, qr,
                                 start=True, stop=False, tile_position=tp))
                        p.op("te", lambda e, res=res, limag=limag, qi=qi,
                             tp=tp, nb2=nb2: e.matmul(
                                 res[:, nb2 * 512:(nb2 + 1) * 512], limag, qi,
                                 start=False, stop=True, tile_position=tp),
                             sig=(1 if nb2 == 1 else 0))
                    t_res = p.cnt["te"]
                    if i2 >= 1:
                        emit_rb(i2 - 1)
                    p.wait("s", "te", t_res)
                    if i2 >= 2:
                        p.wait("s", "te", t_rbmm[i2 - 2])
                    p.op("s", lambda e, res=res, i2=i2: e.activation(
                        rrow(64, 2 * (i2 % 2), 1024), res[64:65, :], AF.Ln,
                        bias=EPS_LIN))
                    t_rcp[i2] = p.op("s", lambda e, i2=i2: e.activation(
                        rowr[64:65, (i2 % 2) * 1024:(i2 % 2) * 1024 + 1024],
                        rrow(64, 2 * (i2 % 2), 1024), AF.Exp,
                        scale=-1.0), sig=1)
                    aoff = attn_off(h) + hf * 1024
                    t_evac[i2] = p.op("s", lambda e, res=res, aoff=aoff:
                        e.activation(A[0:64, aoff:aoff + 1024], res[0:64, :],
                                     AF.Copy), sig=1)
            emit_rb(2 * NH - 1)
            p.barrier(skip=("g",))
            d = dbg("attn0", [64, 4096], BF16)
            if d is not None:
                dma("sy", d, A[0:64, 0:4096])
                p.barrier(skip=("g",))

            wob = [abf(49152, 2048, parts=64), abf(51200, 2048, parts=64)]
            z2b = [abf(53248, 1024), abf(54272, 1024)]
            t_zmm, t_zev, t_zst = {}, {}, {}
            gbar()
            for m in range(NC):
                if m >= 2:
                    p.wait("g", "te", t_zmm[(m - 2, 1)])
                t_wo = dma("g", wob[m % 2].rearrange("p (k c) -> p k c", k=16),
                           wout_t[m].rearrange("k p c -> p k c"), ch=m)
                p.waitt("te", t_wo)
                for hf in range(2):
                    bk0 = 0 if (2 * m + hf) % 2 == 0 else 2
                    if m >= 1:
                        p.wait("te", "s", t_zev.get((m - 1, hf), 0))
                    for kk in range(16):
                        for nb2 in range(2):
                            p.op("te", lambda e, bk0=bk0, kk=kk, nb2=nb2,
                                 hf=hf, m=m: e.matmul(
                                     bank(bk0 + nb2),
                                     wob[m % 2][:, kk * 128:(kk + 1) * 128],
                                     A[0:64, attn_off(kk) + hf * 1024 +
                                       nb2 * 512:attn_off(kk) + hf * 1024 +
                                       nb2 * 512 + 512],
                                     start=(kk == 0), stop=(kk == 15)),
                                 sig=(1 if kk == 15 and nb2 == 1 else 0))
                    t_zmm[(m, hf)] = p.cnt["te"]
                    p.wait("s", "te", t_zmm[(m, hf)])
                    if m >= 1:
                        p.wait("s", "te", t_zst.get((m - 1, hf), 0))
                    p.op("s", lambda e, bk0=bk0, m=m, hf=hf: e.activation(
                        abf(16384 + m * 2048 + hf * 1024, 1024), bank(bk0, 2),
                        AF.Identity, bias=bias_t[:, 56 + m:57 + m]))
                    t_zev[(m, hf)] = p.op("s", lambda e, bk0=bk0, m=m, hf=hf:
                        e.activation(z2b[hf], bank(bk0, 2), AF.Square,
                                     bias=bias_t[:, 56 + m:57 + m]), sig=1)
                    p.wait("te", "s", t_zev[(m, hf)])
                    for nb2 in range(2):
                        p.op("te", lambda e, hf=hf, nb2=nb2, m=m: e.matmul(
                            bank(4 + 2 * hf + nb2)[0:1, :], onesb[:, 0:1],
                            z2b[hf][:, nb2 * 512:(nb2 + 1) * 512],
                            start=(m == 0), stop=(m == NC - 1)),
                            sig=(1 if nb2 == 1 else 0))
                    t_zst[(m, hf)] = p.cnt["te"]
            t_zstat = p.cnt["te"]
            p.wait("s", "te", t_zstat)
            p.op("s", lambda e: e.activation(
                rrow(32, 0, 2048), bank(4, 4)[0:1, :], AF.Ln,
                bias=RMS_EPS, scale=1.0 / D))
            t_inv4 = p.op("s", lambda e: e.activation(
                rowr[0:1, 0:2048], rrow(32, 0, 2048), AF.Exp,
                scale=-0.5), sig=1)
            p.wait("te", "s", t_inv4)
            for nb in range(4):
                p.op("te", lambda e, nb=nb: e.matmul(
                    bank(nb), onesr[0:1, :], rrow_r(0, nb),
                    start=True, stop=True), sig=(1 if nb == 3 else 0))
            t_invb2 = p.cnt["te"]
            p.wait("v", "te", t_invb2)
            t1z = af32(49152, 512)
            for cc2 in range(8):
                for nb in range(4):
                    zs = abf(16384 + cc2 * 2048 + nb * 512, 512)
                    xs = xres[:, cc2 * T + nb * 512:cc2 * T + (nb + 1) * 512]
                    p.op("v", lambda e, zs=zs, nb=nb, cc2=cc2:
                         e.scalar_tensor_tensor(
                             t1z, zs, bias_t[:, 16 + cc2:17 + cc2], bank(nb),
                             ALU.mult, ALU.mult))
                    p.op("v", lambda e, xs=xs: e.tensor_add(xs, xs, t1z),
                         sig=(1 if (cc2 == 7 and nb == 3) else 0))
            p.barrier(skip=("g",))

        if stage >= 2:
            emit_attn()

        if stage >= 3:
            emit_ffn()

        out0 = af32(49152, 1024)
        out1 = af32(51200, 1024)
        t_evo, t_sto = {}, {}
        for lc in range(16):
            buf = out0 if lc % 2 == 0 else out1
            for c in range(NC):
                bk = (lc * NC + c) % 2
                if (lc * NC + c) >= 2:
                    p.wait("te", "s", t_evo[(lc * NC + c) - 2])
                p.op("te", lambda e, bk=bk, c=c, lc=lc: e.transpose(
                    bank(bk)[0:128, 0:128],
                    xres[:, c * T + lc * 128:c * T + (lc + 1) * 128],
                    ident_t[:]), sig=1)
                t_mm = p.cnt["te"]
                if lc >= 2 and c == 0:
                    p.waitt("s", t_sto[lc - 2])
                p.wait("s", "te", t_mm)
                t_evo[lc * NC + c] = p.op("s", lambda e, bk=bk, buf=buf, c=c:
                    e.activation(buf[:, c * 128:(c + 1) * 128],
                                 bank(bk)[0:128, 0:128], AF.Copy), sig=1)
            p.wait("sy", "s", t_evo[lc * NC + NC - 1])
            t_sto[lc] = dma("sy", y_out[lc * 128:(lc + 1) * 128, :], buf, ch=lc)
        p.q["sy"].append(lambda e, t=p.cnt["syA"]: e.wait_ge(p.sem["syA"], t))
        p.q["sy"].append(lambda e, t=p.cnt["syB"]: e.wait_ge(p.sem["syB"], t))

        blk = e_(nc.Block())
        p.run(blk)
        return nc, dbg_outs


def _qperm():
    perm = np.zeros(D, np.int64)
    for c in range(8):
        for pp in range(128):
            h = 4 * (c % 4) + pp // 32
            j = pp % 32
            perm[c * 128 + pp] = 64 * h + 2 * j + (1 if c >= 4 else 0)
    return perm


def _kperm():
    perm = np.zeros(D, np.int64)
    for h in range(NH):
        for r in range(64):
            perm[64 * h + r] = 64 * h + (2 * r if r < 32 else 2 * (r - 32) + 1)
    return perm


def _host_prepare(inputs):
    x = np.asarray(inputs["L0_attn_x"], np.float32)
    wqkv = np.asarray(inputs["L0_attn_Wqkv"], np.float32)
    bqkv = np.asarray(inputs["L0_attn_bqkv"], np.float32)
    gq = np.asarray(inputs["L0_attn_gq"], np.float32)
    gk = np.asarray(inputs["L0_attn_gk"], np.float32)
    wout = np.asarray(inputs["L0_attn_Wout"], np.float32)
    bout = np.asarray(inputs["L0_attn_bout"], np.float32)
    gout = np.asarray(inputs["L0_attn_gout"], np.float32)
    wfc = np.asarray(inputs["L0_ffn_W_fc"], np.float32)
    bfc = np.asarray(inputs["L0_ffn_b_fc"], np.float32)
    afc = np.asarray(inputs["L0_ffn_A_fc"], np.float32)
    bfcm = np.asarray(inputs["L0_ffn_B_fc"], np.float32)
    wpj = np.asarray(inputs["L0_ffn_W_proj"], np.float32)
    bpj = np.asarray(inputs["L0_ffn_b_proj"], np.float32)
    apj = np.asarray(inputs["L0_ffn_A_proj"], np.float32)
    bpjm = np.asarray(inputs["L0_ffn_B_proj"], np.float32)

    qp = _qperm()
    kp = _kperm()
    wqkv_T = np.ascontiguousarray(wqkv.T)
    wqkv_p = np.concatenate([wqkv_T[:, 0:D][:, qp], wqkv_T[:, D:2 * D][:, kp],
                             wqkv_T[:, 2 * D:3 * D]], axis=1)
    wqkv_tiles = np.ascontiguousarray(
        wqkv_p.reshape(NC, 128, 24, 128).transpose(2, 0, 1, 3))
    wout_T = np.ascontiguousarray(wout.T)
    wout_tiles = np.ascontiguousarray(
        wout_T.reshape(16, 64, NC, 128).transpose(2, 0, 1, 3))

    bq_p = bqkv[0:D][qp]
    bk_p = bqkv[D:2 * D][kp]
    bv = bqkv[2 * D:3 * D]
    bias_cols = np.zeros((128, 72), np.float32)
    bias_cols[:, 0:8] = bq_p.reshape(8, 128).T
    bias_cols[:, 8:16] = gq[qp].reshape(8, 128).T
    bias_cols[:, 16:24] = gout.reshape(8, 128).T
    bias_cols[:, 24:56] = bfc.reshape(32, 128).T
    bias_cols[:, 56:64] = bout.reshape(8, 128).T
    bias_cols[:, 64:72] = bpj.reshape(8, 128).T

    rows_f = gk[kp].reshape(1, D)
    rows_b = np.zeros((1, 2048), np.float32)
    rows_b[0, 0:1024] = bk_p
    rows_b[0, 1024:2048] = bv

    j = np.arange(32, dtype=np.float64)
    invf = THETA ** (-j / 32.0)

    def tables(l0):
        t = np.arange(l0, l0 + T, dtype=np.float64)
        ang = np.outer(t, invf)
        cos = np.cos(ang).astype(np.float32)
        sin = np.sin(ang).astype(np.float32)
        cq = np.ascontiguousarray(np.tile(cos.T, (4, 1)))
        sq = np.ascontiguousarray(np.tile(sin.T, (4, 1)))
        ctm = np.zeros((128, 512), np.float32)
        stm = np.zeros((128, 512), np.float32)
        for lc in range(16):
            ctm[:, lc * 32:(lc + 1) * 32] = cos[lc * 128:(lc + 1) * 128, :]
            stm[:, lc * 32:(lc + 1) * 32] = sin[lc * 128:(lc + 1) * 128, :]
        return cq, sq, ctm, stm

    shared = {
        "wqkv_t": wqkv_tiles, "wout_t": wout_tiles,
        "w1t": np.ascontiguousarray(wfc.T), "w2t": np.ascontiguousarray(wpj.T),
        "a_fc": afc, "b_fct": np.ascontiguousarray(bfcm.T),
        "a_pj": apj, "b_pjt": np.ascontiguousarray(bpjm.T),
        "bias_cols": bias_cols, "rows_f": rows_f, "rows_b": rows_b,
        "identf": np.eye(128, dtype=np.float32),
    }
    in_maps = []
    for c in range(8):
        b = c // 2
        l0 = (c % 2) * T
        cq, sq, ctm, stm = tables(l0)
        m = dict(shared)
        m["x"] = np.ascontiguousarray(x[b, l0:l0 + T, :])
        m["cosq"], m["sinq"], m["costm"], m["sintm"] = cq, sq, ctm, stm
        in_maps.append(m)
    return in_maps


def kernel(**inputs):
    key = (_STAGE, tuple(sorted(_DBG)))
    if key not in _CACHE:
        _CACHE[key] = _build(_STAGE)
    nc, dbg_outs = _CACHE[key]
    in_maps = _host_prepare(inputs)
    res = run_bass_kernel_spmd(nc, in_maps, core_ids=list(range(8)))
    out = np.zeros((Bb, Ls, D), np.float32)
    for c in range(8):
        out[c // 2, (c % 2) * T:(c % 2 + 1) * T, :] = res.results[c]["y"]
    if _DBG:
        kernel._dbg_results = [
            {k: res.results[c].get("dbg_" + k) for k in _DBG} for c in range(8)]
    return out


# version 13
# speedup vs baseline: 1.0290x; 1.0164x over previous
import numpy as np
import ml_dtypes
from contextlib import ExitStack

import concourse.bass as bass
import concourse.mybir as mybir
from concourse.bass_utils import run_bass_kernel_spmd

F32 = mybir.dt.float32
F32R = mybir.dt.float32r
BF16 = mybir.dt.bfloat16
AF = mybir.ActivationFunctionType
ALU = mybir.AluOpType

D, Ff, NH, DH, Bb, Ls = 1024, 4096, 16, 64, 4, 4096
T = 2048
NC, NF = 8, 32
RANK = 16
LN_EPS = 1e-5
RMS_EPS = 1.1920929e-07
EPS_LIN = 1e-6
THETA = 10000.0
RMAGIC = 12582912.0

_STAGE = 99
_DBG = {}
_CACHE = {}


class P:
    ENGINES = ("te", "v", "s", "sy", "g")
    CHANNELS = ("syA", "syB", "syC", "syD", "gA", "gB")

    def __init__(self, nc):
        self.nc = nc
        self.q = {e: [] for e in self.ENGINES}
        self.cnt = {e: 0 for e in self.ENGINES}
        self.waited = {}
        self.sem = {}

    def init_sems(self, stack):
        for e in self.ENGINES + self.CHANNELS:
            self.sem[e] = stack.enter_context(self.nc.semaphore("sem_" + e))
            self.cnt.setdefault(e, 0)

    AUTOSYNC = ("v", "s", "g")

    def op(self, eng, fn, sig=0, sig_ch=None):
        ch = sig_ch or eng
        sem = self.sem[ch]
        if ch == eng and eng in self.AUTOSYNC:
            prev = self.cnt[eng]
            self.cnt[eng] += 1
            def run(e, fn=fn, sem=sem, prev=prev):
                e.wait_ge(sem, prev)
                fn(e).then_inc(sem, 1)
            self.q[eng].append(run)
            return self.cnt[eng]
        if sig:
            self.cnt[ch] += sig
            def run(e, fn=fn, sem=sem, sig=sig):
                fn(e).then_inc(sem, sig)
            self.q[eng].append(run)
            return self.cnt[ch]
        self.q[eng].append(lambda e, fn=fn: fn(e))
        return None

    def waitt(self, eng, token):
        if token:
            self.wait(eng, token[0], token[1])

    def wait(self, eng, src, ticket):
        if not ticket or ticket <= 0:
            return
        key = (eng, src)
        if self.waited.get(key, -1) >= ticket:
            return
        self.waited[key] = ticket
        sem = self.sem[src]
        self.q[eng].append(lambda e, sem=sem, t=ticket: e.wait_ge(sem, t))

    def barrier(self, skip=()):
        snap = dict(self.cnt)
        srcs = [s for s in self.ENGINES + self.CHANNELS
                if s not in skip and s not in ("sy", "g")]
        for w in self.ENGINES:
            if w in skip:
                continue
            for s in srcs:
                if w != s:
                    self.wait(w, s, snap[s])

    def run(self, block):
        q = self.q
        @block.sync
        def _(e):
            for fn in q["sy"]:
                fn(e)
        @block.tensor
        def _(e):
            for fn in q["te"]:
                fn(e)
        @block.vector
        def _(e):
            for fn in q["v"]:
                fn(e)
        @block.scalar
        def _(e):
            for fn in q["s"]:
                fn(e)
        @block.gpsimd
        def _(e):
            for fn in q["g"]:
                fn(e)


def _build(stage):
    nc = bass.Bass("TRN2", target_bir_lowering=False, debug=False)

    def din(name, shape, dt=F32):
        return nc.dram_tensor(name, shape, dt, kind="ExternalInput").ap()

    x_in = din("x", [T, D])
    wqkv_t = din("wqkv_t", [24, NC, 128, 128])
    wout_t = din("wout_t", [NC, 16, 64, 128])
    w1t = din("w1t", [D, Ff])
    w2t = din("w2t", [Ff, D])
    a_fc = din("a_fc", [RANK, D])
    b_fct = din("b_fct", [RANK, Ff])
    a_pj = din("a_pj", [RANK, Ff])
    b_pjt = din("b_pjt", [RANK, D])
    bias_cols = din("bias_cols", [128, 72])
    rows_f = din("rows_f", [1, 1024])
    rows_b = din("rows_b", [1, 2048])
    cosq = din("cosq", [128, T])
    sinq = din("sinq", [128, T])
    costm = din("costm", [128, 512])
    sintm = din("sintm", [128, 512])
    identf = din("identf", [128, 128])

    y_out = nc.dram_tensor("y", [T, D], F32, kind="ExternalOutput").ap()

    w1eff = nc.dram_tensor("w1eff", [NF * NC, 128, 128], BF16).ap()
    w2eff = nc.dram_tensor("w2eff", [NC * NF, 128, 128], BF16).ap()
    cc_in = nc.dram_tensor("cc_in", [64, NH * 65], F32).ap()
    cc_out = nc.dram_tensor("cc_out", [64, NH * 65], F32).ap()

    dbg_outs = {}
    def dbg(name, shape, dt=F32):
        if name in _DBG:
            dbg_outs[name] = nc.dram_tensor("dbg_" + name, shape, dt,
                                            kind="ExternalOutput").ap()
            return dbg_outs[name]
        return None

    with ExitStack() as stack:
        e_ = stack.enter_context
        p = P(nc)
        p.init_sems(stack)

        xres = e_(nc.sbuf_tensor("xres", [128, NC * T], F32))
        arena = e_(nc.sbuf_tensor("arena", [128, 58 * 1024], BF16))
        cosq_t = e_(nc.sbuf_tensor("cosq_t", [128, T], BF16))
        sinq_t = e_(nc.sbuf_tensor("sinq_t", [128, T], BF16))
        costm_t = e_(nc.sbuf_tensor("costm_t", [128, 512], BF16))
        sintm_t = e_(nc.sbuf_tensor("sintm_t", [128, 512], BF16))
        ident_t = e_(nc.sbuf_tensor("ident_t", [128, 128], F32))
        onesr = e_(nc.sbuf_tensor("onesr", [128, 128], F32R))
        onesb = e_(nc.sbuf_tensor("onesb", [128, 128], BF16))
        bias_t = e_(nc.sbuf_tensor("bias_t", [128, 72], F32))
        statr = e_(nc.sbuf_tensor("statr", [128, 64], F32))
        rowr = e_(nc.sbuf_tensor("rowr", [128, 2048], F32R))
        t1bt = e_(nc.sbuf_tensor("t1bt", [128, 1536], F32))
        psA = e_(nc.psum_tensor("psA", [128, 2048], F32))
        psB = e_(nc.psum_tensor("psB", [128, 2048], F32))

        def bank(i, w=1):
            t_ = psA if i < 4 else psB
            return t_[:, (i % 4) * 512:((i % 4) + w) * 512]

        A = arena

        def af32(off, n, part0=0, parts=128):
            return A[part0:part0 + parts, off:off + 2 * n].bitcast(F32)

        def abf(off, n, part0=0, parts=128):
            return A[part0:part0 + parts, off:off + n]

        def rrow(part, slot, n=512):
            off = 55296 + slot * 1024
            return A[part:part + 1, off:off + 2 * n].bitcast(F32)

        def rrow_r(part, slot, n=512):
            return rowr[part:part + 1, slot * 512:slot * 512 + n]

        def gbar():
            for s_ in ("te", "v", "s"):
                p.wait("g", s_, p.cnt[s_])

        def dma(qn, out, in_, ch=0, fence=True):
            chn = {"sy": ("syA", "syB"), "sy2": ("syC", "syD"),
                   "g": ("gA", "gB")}[qn]
            name = chn[ch % 2]
            qq = "sy" if qn.startswith("sy") else "g"
            if fence:
                p.wait(qq, name, p.cnt[name])
            t = p.op(qq, lambda e, o=out, i=in_: e.dma_start(out=o, in_=i),
                     sig=16, sig_ch=name)
            return (name, t)

        dma("g", cosq_t[:], cosq)
        dma("g", sinq_t[:], sinq)
        dma("g", costm_t[:], costm)
        dma("g", sintm_t[:], sintm)
        t_idl = dma("sy", ident_t[:], identf)
        t_bc = dma("sy", bias_t[:], bias_cols, ch=1)
        p.waitt("v", t_bc)
        p.op("g", lambda e: e.memset(onesb[:], 1.0))
        p.op("g", lambda e: e.memset(statr[:, 7:8], -RMAGIC))
        p.op("g", lambda e: e.memset(statr[:, 48:49], RMS_EPS))
        nc.const_aps.register(F32, RMS_EPS, statr[:, 48:49])
        p.op("g", lambda e: e.memset(statr[:, 49:50], EPS_LIN))
        nc.const_aps.register(F32, EPS_LIN, statr[:, 49:50])
        t_ms = p.op("g", lambda e: e.memset(statr[:, 6:7], 1.0), sig=1)
        p.wait("v", "g", t_ms)
        p.op("v", lambda e: e.tensor_copy(
            onesr[:], statr[:, 6:7].broadcast_to([128, 128])))
        p.op("v", lambda e: e.tensor_scalar(
            statr[:, 8:16], bias_t[:, 64:72], 0.5, None, ALU.mult), sig=1)

        xtm0 = af32(49152, 1024)
        xtm1 = af32(51200, 1024)
        t_ev = {}
        p.waitt("te", t_idl)
        for lc in range(16):
            buf = xtm0 if lc % 2 == 0 else xtm1
            if lc >= 2:
                p.wait("sy", "s", t_ev[(lc - 2) * NC + NC - 1])
            t_ld = dma("sy", buf, x_in[lc * 128:(lc + 1) * 128, :], ch=lc)
            p.waitt("te", t_ld)
            for c in range(NC):
                bk = (lc * NC + c) % 2
                if (lc * NC + c) >= 2:
                    p.wait("te", "s", t_ev[(lc * NC + c) - 2])
                p.op("te", lambda e, bk=bk, buf=buf, c=c: e.transpose(
                    bank(bk)[0:128, 0:128], buf[:, c * 128:(c + 1) * 128],
                    ident_t[:]), sig=1)
                p.wait("s", "te", p.cnt["te"])
                t_ev[lc * NC + c] = p.op("s", lambda e, bk=bk, c=c, lc=lc:
                    e.activation(
                        xres[:, c * T + lc * 128:c * T + (lc + 1) * 128],
                        bank(bk)[0:128, 0:128], AF.Copy), sig=1)

        def emit_wprep(wT, aT, bT, weff, kin, kout, guard=False,
                       t_redx=None, coissue=None):
            nkc = kin // 128
            av = af32(8192, kin, parts=16)
            bv = af32(16384, kout, parts=16)
            t_a = dma("sy", av, aT)
            t_b = dma("sy", bv, bT, ch=1)
            p.waitt("v", t_a)
            t_a2 = p.op("v", lambda e: e.tensor_scalar(
                av, av, 2.0, None, ALU.mult), sig=1)
            if t_redx is not None:
                t_red = t_redx
            else:
                if guard:
                    p.wait("g", "v", p.cnt["v"])
                    p.wait("g", "s", p.cnt["s"])
                pbufs = [af32(0, 4096), af32(24576, 4096),
                         af32(32768, 4096), af32(40960, 4096)]
                chq = [("sy", 0), ("sy", 1), ("g", 0), ("g", 1)]
                t_red = {}
                for i in range(nkc):
                    qn, cc = chq[i % 4]
                    if i >= 4:
                        p.wait(qn, "v", t_red[i - 4])
                    t_pl = dma(qn, pbufs[i % 4][:, 0:kout],
                               wT[i * 128:(i + 1) * 128, :], ch=cc)
                    p.waitt("v", t_pl)
                    t_red[i] = p.op("v",
                        lambda e, b=pbufs[i % 4], i=i, kout=kout:
                        e.reduce_max(
                            statr[:, 16 + i:17 + i], b[:, 0:kout],
                            axis=mybir.AxisListType.X,
                            apply_absolute_value=True), sig=1)
            p.wait("s", "v", t_red[nkc - 1])
            t_fl = p.op("s", lambda e, nkc=nkc: e.copy(
                statr[:, 16:16 + nkc], statr[:, 16:16 + nkc]), sig=1)
            p.wait("v", "s", t_fl)
            t_pm = p.op("v", lambda e, nkc=nkc: e.reduce_max(
                statr[:, 4:5], statr[:, 16:16 + nkc],
                axis=mybir.AxisListType.X), sig=1)
            p.wait("s", "v", t_pm)
            t_fl2 = p.op("s", lambda e: e.copy(statr[:, 4:5], statr[:, 4:5]),
                         sig=1)
            p.wait("te", "s", t_fl2)
            p.op("te", lambda e: e.transpose(
                bank(0)[0:1, 0:128], statr[:, 4:5], ident_t[:]), sig=1)
            p.wait("v", "te", p.cnt["te"])
            t_gm = p.op("v", lambda e: e.reduce_max(
                rrow(0, 0, 1), bank(0)[0:1, 0:128],
                axis=mybir.AxisListType.X), sig=1)
            p.wait("s", "v", t_gm)
            p.op("s", lambda e: e.mul(
                rrow(0, 1, 1), rrow(0, 0, 1), 1.0 / 127.0))
            p.op("s", lambda e: e.activation(
                rrow(0, 2, 1), rrow(0, 1, 1), AF.Ln))
            t_fl3 = p.op("s", lambda e: e.activation(
                rrow(0, 3, 1), rrow(0, 2, 1), AF.Exp, scale=-1.0), sig=1)
            p.wait("te", "s", t_fl3)
            p.op("te", lambda e: e.matmul(
                bank(0)[:, 0:1], onesr[0:1, :].bitcast(F32), rrow(0, 1, 1),
                start=True, stop=True))
            p.op("te", lambda e: e.matmul(
                bank(1)[:, 0:1], onesr[0:1, :].bitcast(F32), rrow(0, 3, 1),
                start=True, stop=True), sig=1)
            p.wait("s", "te", p.cnt["te"])
            p.op("s", lambda e: e.copy(statr[:, 2:3], bank(0)[:, 0:1]))
            t_stc = p.op("s", lambda e: e.copy(statr[:, 3:4], bank(1)[:, 0:1]),
                         sig=1)
            last01 = max(i for i in range(nkc) if i % 4 < 2)
            p.wait("sy", "v", t_red[last01])
            p.wait("s", "v", t_red[last01])
            p.wait("v", "s", t_stc)
            p.wait("te", "v", t_a2)
            p.waitt("te", t_b)

            pb = [af32(0, 512), af32(1024, 512), af32(26624, 512)]
            pq = [af32(2048, 512), af32(5120, 512), af32(28672, 512)]
            pqm = [af32(3072, 512), af32(6144, 512), af32(29696, 512)]
            wb = [abf(4096, 512), abf(4608, 512)]
            t_ld, t_ts, t_q, t_qm, t_st, t_lor = {}, {}, {}, {}, {}, {}
            nob = kout // 512
            for ob in range(nob):
                for i in range(nkc):
                    it = ob * nkc + i
                    if it >= 3:
                        p.wait("sy", "v", t_ts[it - 3])
                    t_ld[it] = dma("sy", pb[it % 3],
                                   wT[i * 128:(i + 1) * 128,
                                      ob * 512:(ob + 1) * 512], ch=it)
                    bkk = 2 + it % 2
                    if it >= 2:
                        p.wait("te", "v", t_q[it - 2])
                    t_lor[it] = p.op("te", lambda e, bkk=bkk, i=i, ob=ob:
                        e.matmul(bank(bkk),
                                 av[:, i * 128:(i + 1) * 128].bitcast(F32),
                                 bv[:, ob * 512:(ob + 1) * 512].bitcast(F32),
                                 start=True, stop=True), sig=1)
                    p.waitt("v", t_ld[it])
                    if it >= 3:
                        p.wait("v", "s", t_qm[it - 3])
                    t_ts[it] = p.op("v", lambda e, it=it: e.tensor_scalar(
                        pq[it % 3], pb[it % 3], statr[:, 3:4], RMAGIC,
                        ALU.mult, ALU.add), sig=1)
                    p.wait("s", "v", t_ts[it])
                    if it >= 3:
                        p.wait("s", "v", t_q[it - 3])
                    t_qm[it] = p.op("s", lambda e, it=it: e.activation(
                        pqm[it % 3], pq[it % 3], AF.Identity,
                        bias=statr[:, 7:8]), sig=1)
                    p.wait("v", "s", t_qm[it])
                    p.wait("v", "te", t_lor[it])
                    if it >= 2:
                        p.waitt("v", t_st[it - 2])
                    t_q[it] = p.op("v", lambda e, it=it, bkk=bkk:
                        e.scalar_tensor_tensor(
                            wb[it % 2], pqm[it % 3], statr[:, 2:3], bank(bkk),
                            ALU.mult, ALU.add), sig=1)
                    p.wait("g", "v", t_q[it])
                    for mm4 in range(4):
                        m = ob * 4 + mm4
                        t_st[it] = dma("g", weff[m * nkc + i],
                                       wb[it % 2][:, mm4 * 128:(mm4 + 1) * 128],
                                       ch=it, fence=(mm4 == 0))
                    if coissue is not None and it % 2 == 1:
                        coissue(it // 2)
            p.barrier(skip=("g",))

        if stage >= 1:
            co_tred = {}
            pbufs2 = [af32(32768, 1024), af32(34816, 1024)]

            def co_piece(jj):
                if jj == 0:
                    p.wait("sy", "v", p.cnt["v"])
                if jj >= 2:
                    p.wait("sy", "v", co_tred[jj - 2])
                t_pl = dma("sy2", pbufs2[jj % 2],
                           w2t[jj * 128:(jj + 1) * 128, :], ch=jj)
                p.waitt("v", t_pl)
                co_tred[jj] = p.op("v", lambda e, b=pbufs2[jj % 2], jj=jj:
                    e.reduce_max(statr[:, 16 + jj:17 + jj], b,
                                 axis=mybir.AxisListType.X,
                                 apply_absolute_value=True), sig=1)

            emit_wprep(w1t, a_fc, b_fct, w1eff, D, Ff, coissue=co_piece)
            dr = dbg("rows1", [1, 2048])
            if dr is not None:
                dma("sy", dr, A[0:1, 55296:59392].bitcast(F32))
                p.barrier(skip=("g",))
            d0 = dbg("statr1", [128, 64])
            if d0 is not None:
                dma("sy", d0, statr[:])
                p.barrier(skip=("g",))
            emit_wprep(w2t, a_pj, b_pjt, w2eff, Ff, D, t_redx=co_tred)
            d = dbg("w1eff0", [8, 128, 128], BF16)
            if d is not None:
                dma("sy", d, w1eff[0:8])
                p.barrier(skip=("g",))

        t1b = [t1bt[:, 0:512], t1bt[:, 512:1024], t1bt[:, 1024:1536]]

        def emit_ln(nblocks, lnb_slice, bs1, bs2, bab, bcb):
            xb = [abf(51200 + i * 512, 512) for i in range(4)]
            x2b = [abf(53248 + i * 512, 512) for i in range(4)]
            t_fin_prev = {0: 0, 1: 0}
            t_ap_prev = {0: 0, 1: 0}
            t_bc_prev = {0: 0, 1: 0}
            t_f4_prev = {0: 0, 1: 0}
            prev_s1mm = {0: {}, 1: {}}
            prev_s2mm = {0: {}, 1: {}}
            for inb, nb in enumerate(nblocks):
                par = inb % 2
                rp = 32 * par
                b1, b2, ba, bc = (bs1, bs2, bab, bcb) if par == 0 else \
                    (bs1 - 4, bs2 - 4, bab - 4, bcb - 4)
                onesr_p = onesr[rp:rp + 1, :]
                t_cast, t_sq, t_s1mm, t_s2mm = {}, {}, {}, {}
                p.wait("te", "v", t_fin_prev[par])
                for c in range(NC):
                    xs = xres[:, c * T + nb * 512:c * T + (nb + 1) * 512]
                    bi = 2 * par + (c % 2)
                    if c >= 2:
                        p.wait("v", "te", t_s1mm[c - 2])
                        p.wait("s", "te", t_s2mm[c - 2])
                    elif prev_s1mm[par]:
                        p.wait("v", "te", prev_s1mm[par][NC - 2 + c])
                        p.wait("s", "te", prev_s2mm[par][NC - 2 + c])
                    t_cast[c] = p.op("v", lambda e, xs=xs, bi=bi: e.tensor_copy(
                        xb[bi], xs), sig=1)
                    t_sq[c] = p.op("s", lambda e, xs=xs, bi=bi: e.activation(
                        x2b[bi], xs, AF.Square), sig=1)
                    p.wait("te", "v", t_cast[c])
                    p.wait("te", "s", t_sq[c])
                    t_s1mm[c] = p.op("te", lambda e, c=c, bi=bi, b1=b1: e.matmul(
                        bank(b1)[0:1, :], onesb[:, 0:1], xb[bi],
                        start=(c == 0), stop=(c == NC - 1)), sig=1)
                    t_s2mm[c] = p.op("te", lambda e, c=c, bi=bi, b2=b2: e.matmul(
                        bank(b2)[0:1, :], onesb[:, 0:1], x2b[bi],
                        start=(c == 0), stop=(c == NC - 1)), sig=1)
                p.wait("v", "te", t_s2mm[NC - 1])
                p.wait("v", "s", t_f4_prev[par])
                p.op("v", lambda e, rp=rp, b1=b1: e.tensor_scalar(
                    rrow(rp, 0), bank(b1)[0:1, :], 1.0 / D, None, ALU.mult))
                t_f1 = p.op("v", lambda e, rp=rp, b2=b2: e.tensor_scalar(
                    rrow(rp, 3), bank(b2)[0:1, :], 1.0 / D, LN_EPS, ALU.mult,
                    ALU.add), sig=1)
                t_fin_prev[par] = t_f1
                p.wait("s", "v", t_f1)
                t_f2 = p.op("s", lambda e, rp=rp: e.activation(
                    rrow(rp, 1), rrow(rp, 0), AF.Square), sig=1)
                p.wait("v", "s", t_f2)
                t_f3 = p.op("v", lambda e, rp=rp: e.tensor_sub(
                    rrow(rp, 3), rrow(rp, 3), rrow(rp, 1)), sig=1)
                p.wait("s", "v", t_f3)
                t_ln = p.op("s", lambda e, rp=rp: e.activation(
                    rrow(rp, 3), rrow(rp, 3), AF.Ln), sig=1)
                p.wait("v", "s", t_ln)
                t_cpv = p.op("v", lambda e, rp=rp: e.tensor_copy(
                    rrow(rp, 1), rrow(rp, 3)), sig=1)
                p.wait("s", "v", t_cpv)
                p.wait("s", "te", t_bc_prev[par])
                t_f4 = p.op("s", lambda e, rp=rp: e.activation(
                    rrow_r(rp, 1), rrow(rp, 1), AF.Exp, scale=-0.5), sig=1)
                t_f4_prev[par] = t_f4
                p.wait("v", "s", t_f4)
                p.wait("v", "te", t_bc_prev[par])
                t_f5 = p.op("v", lambda e, rp=rp: e.scalar_tensor_tensor(
                    rrow_r(rp, 2), rrow(rp, 0), -1.0,
                    rrow_r(rp, 1).bitcast(F32), ALU.mult, ALU.mult), sig=1)
                p.wait("te", "v", t_f5)
                p.wait("te", "v", t_ap_prev[par])
                p.op("te", lambda e, rp=rp, ba=ba, o_=onesr_p: e.matmul(
                    bank(ba), o_, rrow_r(rp, 1), start=True, stop=True))
                t_bc2 = p.op("te", lambda e, rp=rp, bc=bc, o_=onesr_p: e.matmul(
                    bank(bc), o_, rrow_r(rp, 2), start=True,
                    stop=True), sig=1)
                t_bc_prev[par] = t_bc2
                p.wait("v", "te", t_bc2)
                for c in range(NC):
                    xs = xres[:, c * T + nb * 512:c * T + (nb + 1) * 512]
                    p.op("v", lambda e, xs=xs, c=c, ba=ba: e.tensor_mul(
                        t1b[c % 3], xs, bank(ba)))
                    if c >= 1:
                        p.op("v", lambda e, c=c, nb=nb, bc=bc: e.tensor_add(
                            lnb_slice(c - 1, nb), t1b[(c - 1) % 3], bank(bc)))
                t_ap_prev[par] = p.op("v", lambda e, nb=nb, bc=bc: e.tensor_add(
                    lnb_slice(NC - 1, nb), t1b[(NC - 1) % 3], bank(bc)), sig=1)
                prev_s1mm[par], prev_s2mm[par] = t_s1mm, t_s2mm

        def emit_ffn():
            for H in (0, 1):
                def lnb_slice(c, nb, H=H):
                    return abf(32768 + c * 1024 + (nb - 2 * H) * 512, 512)
                emit_ln([2 * H, 2 * H + 1], lnb_slice, 4, 5, 6, 7)
                p.barrier(skip=("g",))
                if H == 0:
                    d = dbg("ln1b", [128, 8192], BF16)
                    if d is not None:
                        dma("sy", d, abf(32768, 8192))
                        p.barrier(skip=("g",))

                w1b = [abf(49152, 1024), abf(50176, 1024), abf(51200, 1024)]
                t_fcmm, t_gel = {}, {}
                for m in range(NF):
                    if m >= 3:
                        p.wait("sy", "te", t_fcmm[m - 3])
                    t_wld = dma("sy",
                                w1b[m % 3].rearrange("p (k c) -> p k c", k=8),
                                w1eff[m * 8:(m + 1) * 8]
                                .rearrange("k p c -> p k c"), ch=m)
                    p.waitt("te", t_wld)
                    if m >= 2:
                        p.wait("te", "s", t_gel[m - 2])
                    bk0 = 0 if m % 2 == 0 else 2
                    for k in range(NC):
                        for nb in range(2):
                            p.op("te", lambda e, bk0=bk0, k=k, nb=nb, m=m:
                                 e.matmul(bank(bk0 + nb),
                                          w1b[m % 3][:, k * 128:(k + 1) * 128],
                                          abf(32768 + k * 1024 + nb * 512, 512),
                                          start=(k == 0), stop=(k == NC - 1)),
                                 sig=(1 if (k == NC - 1 and nb == 1) else 0))
                    t_fcmm[m] = p.cnt["te"]
                    p.wait("s", "te", t_fcmm[m])
                    t_gel[m] = p.op("s", lambda e, bk0=bk0, m=m: e.activation(
                        abf(m * 1024, 1024), bank(bk0, 2), AF.Gelu,
                        bias=bias_t[:, 24 + m:25 + m]), sig=1)

                if H == 0:
                    d = dbg("hblk", [128, 8192], BF16)
                    if d is not None:
                        p.barrier(skip=("g",))
                        dma("sy", d, abf(0, 8192))
                        p.barrier(skip=("g",))
                w2b = [abf(40960, 4096), abf(45056, 4096)]
                t_pjmm, t_pj = {}, {}
                for m in range(NC):
                    if m >= 2:
                        p.wait("sy", "te", t_pjmm[m - 2])
                    t_w2 = dma("sy",
                               w2b[m % 2].rearrange("p (k c) -> p k c", k=32),
                               w2eff[m * 32:(m + 1) * 32]
                               .rearrange("k p c -> p k c"), ch=m)
                    p.waitt("te", t_w2)
                    if m >= 2:
                        p.wait("te", "v", t_pj[m - 2])
                    bk0 = 0 if m % 2 == 0 else 2
                    if m < 2:
                        p.wait("te", "s", t_gel[NF - 2 + m])
                    for k in range(NF):
                        p.wait("te", "s", t_gel[k])
                        for nb in range(2):
                            p.op("te", lambda e, bk0=bk0, k=k, nb=nb, m=m:
                                 e.matmul(bank(bk0 + nb),
                                          w2b[m % 2][:, k * 128:(k + 1) * 128],
                                          abf(k * 1024 + nb * 512, 512),
                                          start=(k == 0), stop=(k == NF - 1)),
                                 sig=(1 if (k == NF - 1 and nb == 1) else 0))
                    t_pjmm[m] = p.cnt["te"]
                    p.wait("v", "te", t_pjmm[m])
                    xs = xres[:, m * T + H * 1024:m * T + (H + 1) * 1024]
                    t_pj[m] = p.op("v", lambda e, bk0=bk0, xs=xs:
                        e.scalar_tensor_tensor(
                            xs, bank(bk0, 2), 0.5, xs, ALU.mult, ALU.add), sig=1)
                for m in range(NC):
                    p.wait("s", "v", t_pj[m])
                    xs = xres[:, m * T + H * 1024:m * T + (H + 1) * 1024]
                    p.op("s", lambda e, xs=xs, m=m: e.activation(
                        xs, xs, AF.Identity, bias=statr[:, 8 + m:9 + m]),
                        sig=(1 if m == NC - 1 else 0))
                p.barrier(skip=("g",))

        if stage >= 1:
            emit_ffn()

        def emit_attn():
            gbar()
            gkbc = af32(16384, 1024)
            bkbc = abf(18432, 1024)
            bvbc = abf(19456, 1024)
            rowbk = A[0:1, 20480:21504]
            rowbv = A[0:1, 21504:22528]
            vkacc = af32(22528, 1040, parts=64)
            t_rf = dma("sy", A[0:1, 55296:55296 + 2048].bitcast(F32), rows_f)
            t_rb = dma("g", A[0:1, 20480:22528], rows_b)
            p.waitt("v", t_rf)
            t_rr = p.op("v", lambda e: e.tensor_copy(
                rowr[0:1, 0:1024],
                A[0:1, 55296:55296 + 2048].bitcast(F32)), sig=1)
            p.wait("te", "v", t_rr)
            p.waitt("te", t_rb)
            for nb in range(2):
                p.op("te", lambda e, nb=nb: e.matmul(
                    bank(0 + nb), onesr[0:1, :],
                    rowr[0:1, nb * 512:(nb + 1) * 512],
                    start=True, stop=True))
                p.op("te", lambda e, nb=nb: e.matmul(
                    bank(2 + nb), onesb[0:1, :],
                    rowbk[0:1, nb * 512:(nb + 1) * 512], start=True, stop=True))
                p.op("te", lambda e, nb=nb: e.matmul(
                    bank(4 + nb), onesb[0:1, :],
                    rowbv[0:1, nb * 512:(nb + 1) * 512], start=True,
                    stop=True), sig=(1 if nb == 1 else 0))
            t_bcmm = p.cnt["te"]
            p.wait("v", "te", t_bcmm)
            p.op("v", lambda e: e.tensor_copy(gkbc, psA[:, 0:1024]))
            p.op("v", lambda e: e.tensor_copy(bkbc, psA[:, 1024:2048]))
            p.op("v", lambda e: e.tensor_copy(bvbc, psB[:, 0:1024]))
            p.op("v", lambda e: e.memset(vkacc, 0.0), sig=1)
            p.barrier(skip=("g",))

            def ln2_slice(c, nb):
                return abf(c * 2048 + nb * 512, 512)
            emit_ln([0, 1, 2, 3], ln2_slice, 4, 5, 6, 7)
            p.barrier(skip=("g",))
            d = dbg("ln2b", [128, 16384], BF16)
            if d is not None:
                dma("sy", d, abf(0, 16384))
                p.barrier(skip=("g",))

            wkvA = abf(32768, 4096)
            wkvB = abf(49152, 4096)
            k_sb = abf(36864, 2048)
            kf = [abf(38912 + i * 1024, 1024) for i in range(4)]
            ktl = abf(43008, 1024)
            u1 = abf(44032, 512)
            u2 = abf(44544, 512)
            vp = [abf(45056 + i * 1024, 1024) for i in range(4)]
            t_kvld, t_kvmm, t_kvev, t_rope, t_vkadd = {}, {}, {}, {}, {}
            t_vkmm2, t_k2d = {}, {}
            gbar()

            def prev_evkey(g, bb2, lq):
                if bb2 > 0:
                    return (g, bb2 - 1, lq)
                if g > 0:
                    return (g - 1, 3, lq)
                return None

            def emit_vk(g):
                for lq in range(2):
                    lc = 2 * g + lq
                    kfb = kf[lc % 4]
                    p.wait("te", "s", t_rope[lc])
                    p.wait("te", "v", t_kvev[(g, 3, lq)])
                    p.wait("te", "v", t_vkadd.get(lc - 1, 0))
                    for h in range(NH):
                        colv = [512 + 65 * h, 1024 + 65 * (h - 7),
                                1536 + 65 * (h - 14)][h // 7]
                        vks = psB[0:64, colv:colv + 65]
                        p.op("te", lambda e, h=h, kfb=kfb, lc=lc, vks=vks:
                             e.matmul(vks[:, 0:64],
                                      kfb[:, 64 * h:64 * h + 64],
                                      vp[lc % 4][:, 64 * h:64 * h + 64],
                                      start=True, stop=True), sig=1)
                        p.op("te", lambda e, h=h, kfb=kfb, vks=vks:
                             e.matmul(vks[:, 64:65],
                                      kfb[:, 64 * h:64 * h + 64],
                                      onesb[:, 0:1],
                                      start=True, stop=True), sig=1)
                    t_vkh = p.cnt["te"]
                    t_vkmm2[(g, lq)] = t_vkh
                    p.wait("v", "te", t_vkh)
                    p.op("v", lambda e: e.tensor_add(
                        vkacc[:, 0:455], vkacc[:, 0:455],
                        psB[0:64, 512:512 + 455]))
                    p.op("v", lambda e: e.tensor_add(
                        vkacc[:, 455:910], vkacc[:, 455:910],
                        psB[0:64, 1024:1024 + 455]))
                    t_vkadd[lc] = p.op("v", lambda e: e.tensor_add(
                        vkacc[:, 910:1040], vkacc[:, 910:1040],
                        psB[0:64, 1536:1536 + 130]), sig=1)

            for g in range(8):
                for bb2 in range(4):
                    it = g * 4 + bb2
                    wb = wkvA if it % 2 == 0 else wkvB
                    m0 = 8 + 4 * bb2
                    if it >= 2:
                        p.wait("g", "te", t_kvmm[it - 2])
                    tl = None
                    for kk in range(NC):
                        tl = dma("g",
                                 wb[:, kk * 512:(kk + 1) * 512]
                                 .rearrange("p (a c) -> p a c", a=4),
                                 wqkv_t[m0:m0 + 4, kk]
                                 .rearrange("a p c -> p a c"), ch=it,
                                 fence=(kk == 0))
                    t_kvld[it] = tl
                    p.waitt("te", t_kvld[it])
                    for kk in range(NC):
                        for lq in range(2):
                            lc = 2 * g + lq
                            if kk == 0:
                                pk = prev_evkey(g, bb2, lq)
                                if pk is not None:
                                    p.wait("te", "v", t_kvev.get(pk, 0))
                            p.op("te", lambda e, kk=kk, lq=lq, lc=lc, wb=wb:
                                 e.matmul(bank(lq),
                                          A[:, kk * 2048 + lc * 128:
                                            kk * 2048 + lc * 128 + 128],
                                          wb[:, kk * 512:(kk + 1) * 512],
                                          start=(kk == 0), stop=(kk == NC - 1)),
                                 sig=(1 if kk == NC - 1 else 0))
                    t_kvmm[it] = p.cnt["te"]
                    p.wait("v", "te", t_kvmm[it])
                    if bb2 < 2:
                        for lq in range(2):
                            if g >= 1 and bb2 == 0:
                                p.wait("v", "s", t_k2d.get((g - 1, lq), 0))
                            t_kvev[(g, bb2, lq)] = p.op("v",
                                lambda e, lq=lq, bb2=bb2: e.tensor_tensor(
                                    k_sb[:, lq * 1024 + bb2 * 512:
                                         lq * 1024 + bb2 * 512 + 512],
                                    bank(lq),
                                    bkbc[:, bb2 * 512:(bb2 + 1) * 512],
                                    ALU.add), sig=1)
                    if bb2 == 1:
                        for lq in range(2):
                            lc = 2 * g + lq
                            kfb = kf[lc % 4]
                            p.wait("s", "v", t_kvev[(g, 1, lq)])
                            if g >= 2:
                                p.wait("s", "te", t_vkmm2.get((g - 2, lq), 0))
                            t_k2 = p.op("s", lambda e, lq=lq, kfb=kfb:
                                e.activation(kfb,
                                             k_sb[:, lq * 1024:(lq + 1) * 1024],
                                             AF.Square,
                                             accum_out=statr[:, 0:1]), sig=1)
                            t_k2d[(g, lq)] = t_k2
                            p.op("s", lambda e: e.activation(
                                statr[:, 1:2], statr[:, 0:1], AF.Ln,
                                bias=RMS_EPS, scale=1.0 / D))
                            t_i2 = p.op("s", lambda e: e.activation(
                                statr[:, 1:2], statr[:, 1:2], AF.Exp,
                                scale=-0.5), sig=1)
                            p.wait("v", "s", t_i2)
                            p.op("v", lambda e, lq=lq: e.scalar_tensor_tensor(
                                ktl, k_sb[:, lq * 1024:(lq + 1) * 1024],
                                statr[:, 1:2], gkbc, ALU.mult, ALU.mult))
                            rp = lambda t_: t_.rearrange(
                                "p (h d) -> p h d", d=64)[:, :, 0:32]
                            ip = lambda t_: t_.rearrange(
                                "p (h d) -> p h d", d=64)[:, :, 32:64]
                            cpat = costm_t[:, lc * 32:(lc + 1) * 32] \
                                .unsqueeze(1).broadcast_to([128, 16, 32])
                            spat = sintm_t[:, lc * 32:(lc + 1) * 32] \
                                .unsqueeze(1).broadcast_to([128, 16, 32])
                            u1p = u1.rearrange("p (h d) -> p h d", d=32)
                            u2p = u2.rearrange("p (h d) -> p h d", d=32)
                            p.op("v", lambda e, cpat=cpat: e.tensor_tensor(
                                u1p, rp(ktl), cpat, ALU.mult))
                            p.op("v", lambda e, spat=spat: e.tensor_tensor(
                                u2p, rp(ktl), spat, ALU.mult))
                            p.op("v", lambda e, spat=spat, kfb=kfb:
                                 e.tensor_tensor(rp(kfb), ip(ktl), spat,
                                                 ALU.mult))
                            p.op("v", lambda e, kfb=kfb: e.tensor_tensor(
                                rp(kfb), u1p, rp(kfb), ALU.subtract))
                            p.op("v", lambda e, cpat=cpat, kfb=kfb:
                                 e.tensor_tensor(ip(kfb), ip(ktl), cpat,
                                                 ALU.mult))
                            t_rp6 = p.op("v", lambda e, kfb=kfb:
                                e.tensor_tensor(ip(kfb), u2p, ip(kfb),
                                                ALU.add), sig=1)
                            p.wait("s", "v", t_rp6)
                            t_rope[lc] = p.op("s", lambda e, kfb=kfb:
                                e.activation(kfb, kfb, AF.Relu), sig=1)
                    if bb2 >= 2:
                        for lq in range(2):
                            lc = 2 * g + lq
                            if g >= 2 and bb2 == 2:
                                p.wait("v", "te", t_vkmm2.get((g - 2, lq), 0))
                            t_kvev[(g, bb2, lq)] = p.op("v",
                                lambda e, lc=lc, bb2=bb2, lq=lq: e.tensor_tensor(
                                    vp[lc % 4][:, (bb2 - 2) * 512:(bb2 - 1) * 512],
                                    bank(lq),
                                    bvbc[:, (bb2 - 2) * 512:(bb2 - 1) * 512],
                                    ALU.add), sig=1)
                if g >= 1:
                    emit_vk(g - 1)
            emit_vk(7)
            t_a2_te = p.cnt["te"]
            d = dbg("kf15", [128, 1024], BF16)
            if d is not None:
                p.barrier(skip=("g",))
                dma("sy", d, kf[1])
                d2 = dbg("vp15", [128, 1024], BF16)
                dma("sy", d2, vp[1], ch=1)
                d3 = dbg("ksb15", [128, 1024], BF16)
                dma("sy", d3, k_sb[:, 1024:2048])
                p.barrier(skip=("g",))

            p.wait("sy", "v", t_vkadd[T // 128 - 1])
            d = dbg("vkacc", [64, 1040])
            if d is not None:
                dma("sy", d, vkacc)
            t_cci = dma("sy", cc_in, vkacc)
            p.waitt("g", t_cci)
            t_cc = p.op("g", lambda e: e.collective_compute(
                "AllReduce", ALU.add,
                replica_groups=[[0, 1], [2, 3], [4, 5], [6, 7]],
                ins=[cc_in.opt()], outs=[cc_out.opt()]), sig=1)
            vkrep = af32(53248, 520)
            vkrepb = abf(54288, 520)
            t_sc2 = {}
            for a in range(4):
                inr = cc_out[0:32, :].rearrange(
                    "p (j x) -> p j x", x=260)[:, :, 65 * a:65 * a + 65]
                ini = cc_out[32:64, :].rearrange(
                    "p (j x) -> p j x", x=260)[:, :, 65 * a:65 * a + 65]
                outr = vkrep[32 * a:32 * a + 32, 0:260].rearrange(
                    "p (j x) -> p j x", x=65)
                outi = vkrep[32 * a:32 * a + 32, 260:520].rearrange(
                    "p (j x) -> p j x", x=65)
                p.wait("g", "g", t_cc)
                dma("g", outr, inr, ch=1, fence=(a == 0))
                t_sc2[a] = dma("g", outi, ini, ch=1, fence=False)
            d = dbg("vkrep", [128, 520])
            if d is not None:
                p.waitt("sy", t_sc2[3])
                dma("sy", d, vkrep)

            wqb = [abf(32768, 1024), abf(33792, 1024)]
            q2b = [abf(34816, 1024), abf(35840, 1024)]
            wqs = [af32(36864, 1024), af32(38912, 1024)]
            t_qmm, t_qev, t_qst, t_cast = {}, {}, {}, {}
            p.waitt("s", t_cci)
            p.wait("sy", "te", t_a2_te)
            p.wait("sy", "s", t_k2d[(7, 1)])
            p.wait("sy", "v", p.cnt["v"])
            p.wait("v", "te", t_a2_te)
            for m in range(NC):
                if m >= 2:
                    p.wait("sy", "v", t_cast[m - 2])
                t_wq = dma("sy", wqs[m % 2].rearrange("p (k c) -> p k c", k=8),
                           wqkv_t[m].rearrange("k p c -> p k c"), ch=m)
                p.waitt("v", t_wq)
                if m >= 2:
                    p.wait("v", "te", t_qmm[(m - 2, 1)])
                t_cast[m] = p.op("v", lambda e, m=m: e.tensor_copy(
                    wqb[m % 2], wqs[m % 2]), sig=1)
                p.wait("te", "v", t_cast[m])
                for hf in range(2):
                    bk0 = 0 if (2 * m + hf) % 2 == 0 else 2
                    if m >= 1:
                        p.wait("te", "s", t_qev.get((m - 1, hf), 0))
                    for kk in range(NC):
                        for nb2 in range(2):
                            p.op("te", lambda e, bk0=bk0, kk=kk, nb2=nb2,
                                 hf=hf, m=m: e.matmul(
                                     bank(bk0 + nb2),
                                     wqb[m % 2][:, kk * 128:(kk + 1) * 128],
                                     abf(kk * 2048 + hf * 1024 + nb2 * 512, 512),
                                     start=(kk == 0), stop=(kk == NC - 1)),
                                 sig=(1 if kk == NC - 1 and nb2 == 1 else 0))
                    t_qmm[(m, hf)] = p.cnt["te"]
                    p.wait("s", "te", t_qmm[(m, hf)])
                    if m >= 1:
                        p.wait("s", "te", t_qst.get((m - 1, hf), 0))
                    p.op("s", lambda e, bk0=bk0, m=m, hf=hf: e.activation(
                        abf(16384 + m * 2048 + hf * 1024, 1024), bank(bk0, 2),
                        AF.Identity, bias=bias_t[:, m:m + 1]))
                    t_qev[(m, hf)] = p.op("s", lambda e, bk0=bk0, m=m, hf=hf:
                        e.activation(q2b[hf], bank(bk0, 2), AF.Square,
                                     bias=bias_t[:, m:m + 1]), sig=1)
                    p.wait("te", "s", t_qev[(m, hf)])
                    for nb2 in range(2):
                        p.op("te", lambda e, hf=hf, nb2=nb2, m=m: e.matmul(
                            bank(4 + 2 * hf + nb2)[0:1, :], onesb[:, 0:1],
                            q2b[hf][:, nb2 * 512:(nb2 + 1) * 512],
                            start=(m == 0), stop=(m == NC - 1)),
                            sig=(1 if nb2 == 1 else 0))
                    t_qst[(m, hf)] = p.cnt["te"]
            t_qstat = p.cnt["te"]
            p.waitt("v", t_sc2[3])
            t_vkb = p.op("v", lambda e: e.tensor_copy(vkrepb, vkrep), sig=1)

            p.wait("s", "te", t_qstat)
            p.op("s", lambda e: e.activation(
                rrow(32, 0, 2048), bank(4, 4)[0:1, :], AF.Ln,
                bias=RMS_EPS, scale=1.0 / D))
            t_inv3 = p.op("s", lambda e: e.activation(
                rowr[0:1, 0:2048], rrow(32, 0, 2048), AF.Exp,
                scale=-0.5), sig=1)

            scr1 = abf(32768, 2048)
            scr2 = abf(34816, 2048)
            scr3 = abf(36864, 2048)
            t_qrope = {}
            p.wait("v", "s", t_qev[(NC - 1, 1)])
            p.wait("v", "te", t_qstat)
            for c in range(4):
                qr = abf(16384 + c * 2048, 2048)
                qi = abf(16384 + (4 + c) * 2048, 2048)
                p.op("v", lambda e, qr=qr: e.tensor_tensor(
                    scr1, qr, cosq_t[:], ALU.mult))
                p.op("v", lambda e, qr=qr: e.tensor_tensor(
                    scr2, qr, sinq_t[:], ALU.mult))
                p.op("v", lambda e, qi=qi: e.tensor_tensor(
                    scr3, qi, sinq_t[:], ALU.mult))
                p.op("v", lambda e, qr=qr: e.tensor_tensor(
                    qr, scr1, scr3, ALU.subtract))
                p.op("v", lambda e, qi=qi: e.tensor_tensor(
                    scr3, qi, cosq_t[:], ALU.mult))
                t_r6 = p.op("v", lambda e, qi=qi: e.tensor_tensor(
                    qi, scr2, scr3, ALU.add), sig=1)
                p.wait("s", "v", t_r6)
                p.op("s", lambda e, qr=qr: e.activation(qr, qr, AF.Relu))
                t_qrope[c] = p.op("s", lambda e, qi=qi: e.activation(
                    qi, qi, AF.Relu), sig=1)
            p.wait("te", "s", t_inv3)
            p.wait("te", "s", t_qrope[3])
            for nb in range(4):
                p.op("te", lambda e, nb=nb: e.matmul(
                    bank(nb), onesr[0:1, :], rrow_r(0, nb),
                    start=True, stop=True), sig=(1 if nb == 3 else 0))
            t_invb = p.cnt["te"]
            p.wait("v", "te", t_invb)
            t_qsc = 0
            p.waitt("v", t_cci)
            for cc2 in range(8):
                for nb in range(4):
                    qs = abf(16384 + cc2 * 2048 + nb * 512, 512)
                    t_qsc = p.op("v", lambda e, qs=qs, nb=nb, cc2=cc2:
                        e.scalar_tensor_tensor(
                            qs, qs, bias_t[:, 8 + cc2:9 + cc2], bank(nb),
                            ALU.mult, ALU.mult),
                        sig=(1 if (cc2 == 7 and nb == 3) else 0))
            p.barrier(skip=("g",))
            d = dbg("qf", [128, 16384], BF16)
            if d is not None:
                dma("sy", d, abf(16384, 16384))
                p.barrier(skip=("g",))

            def attn_off(h):
                return h * 2048 if h < 8 else 32768 + (h - 8) * 2048

            t_evac, t_rcp, t_rbmm, t_div = {}, {}, {}, {}

            def emit_rb(i2):
                rbb = bank((4, 6)[i2 % 2], 2)[0:64, :]
                p.wait("te", "s", t_rcp[i2])
                if i2 >= 2:
                    p.wait("te", "v", t_div[i2 - 2])
                for nb2 in range(2):
                    p.op("te", lambda e, rbb=rbb, i2=i2, nb2=nb2: e.matmul(
                        rbb[:, nb2 * 512:(nb2 + 1) * 512],
                        onesr[64:65, 0:64],
                        rowr[64:65, (i2 % 2) * 1024 + nb2 * 512:
                             (i2 % 2) * 1024 + nb2 * 512 + 512],
                        start=True, stop=True, tile_position=(64, 0)),
                        sig=(1 if nb2 == 1 else 0))
                t_rbmm[i2] = p.cnt["te"]
                p.wait("v", "te", t_rbmm[i2])
                p.wait("v", "s", t_evac[i2])
                aoff = attn_off(i2 // 2) + (i2 % 2) * 1024
                t_div[i2] = p.op("v", lambda e, rbb=rbb, aoff=aoff:
                    e.tensor_tensor(
                        A[0:64, aoff:aoff + 1024],
                        A[0:64, aoff:aoff + 1024], rbb, ALU.mult), sig=1)

            p.wait("te", "v", t_vkb)
            for h in range(NH):
                a = h % 4
                j = h // 4
                tp = (96, 0) if a == 3 else None
                lreal = vkrepb[32 * a:32 * a + 32, 65 * j:65 * j + 65]
                limag = vkrepb[32 * a:32 * a + 32,
                               260 + 65 * j:260 + 65 * j + 65]
                for hf in range(2):
                    i2 = 2 * h + hf
                    bres = 0 if i2 % 2 == 0 else 2
                    res = bank(bres, 2)[0:65, :]
                    if i2 >= 2:
                        p.wait("te", "s", t_evac[i2 - 2])
                    for nb2 in range(2):
                        qoff = 16384 + (h // 4) * 2048 + hf * 1024 + nb2 * 512
                        qr = A[32 * a:32 * a + 32, qoff:qoff + 512]
                        qi = A[32 * a:32 * a + 32,
                               qoff + 8192:qoff + 8192 + 512]
                        p.op("te", lambda e, res=res, lreal=lreal, qr=qr,
                             tp=tp, nb2=nb2: e.matmul(
                                 res[:, nb2 * 512:(nb2 + 1) * 512], lreal, qr,
                                 start=True, stop=False, tile_position=tp))
                        p.op("te", lambda e, res=res, limag=limag, qi=qi,
                             tp=tp, nb2=nb2: e.matmul(
                                 res[:, nb2 * 512:(nb2 + 1) * 512], limag, qi,
                                 start=False, stop=True, tile_position=tp),
                             sig=(1 if nb2 == 1 else 0))
                    t_res = p.cnt["te"]
                    if i2 >= 1:
                        emit_rb(i2 - 1)
                    p.wait("s", "te", t_res)
                    if i2 >= 2:
                        p.wait("s", "te", t_rbmm[i2 - 2])
                    p.op("s", lambda e, res=res, i2=i2: e.activation(
                        rrow(64, 2 * (i2 % 2), 1024), res[64:65, :], AF.Ln,
                        bias=EPS_LIN))
                    t_rcp[i2] = p.op("s", lambda e, i2=i2: e.activation(
                        rowr[64:65, (i2 % 2) * 1024:(i2 % 2) * 1024 + 1024],
                        rrow(64, 2 * (i2 % 2), 1024), AF.Exp,
                        scale=-1.0), sig=1)
                    aoff = attn_off(h) + hf * 1024
                    t_evac[i2] = p.op("s", lambda e, res=res, aoff=aoff:
                        e.activation(A[0:64, aoff:aoff + 1024], res[0:64, :],
                                     AF.Copy), sig=1)
            emit_rb(2 * NH - 1)
            p.barrier(skip=("g",))
            d = dbg("attn0", [64, 4096], BF16)
            if d is not None:
                dma("sy", d, A[0:64, 0:4096])
                p.barrier(skip=("g",))

            wob = [abf(49152, 2048, parts=64), abf(51200, 2048, parts=64)]
            z2b = [abf(53248, 1024), abf(54272, 1024)]
            t_zmm, t_zev, t_zst = {}, {}, {}
            p.wait("g", "te", t_a2_te)
            for m in range(NC):
                if m >= 2:
                    p.wait("g", "te", t_zmm[(m - 2, 1)])
                t_wo = dma("g", wob[m % 2].rearrange("p (k c) -> p k c", k=16),
                           wout_t[m].rearrange("k p c -> p k c"), ch=m)
                p.waitt("te", t_wo)
                for hf in range(2):
                    bk0 = 0 if (2 * m + hf) % 2 == 0 else 2
                    if m >= 1:
                        p.wait("te", "s", t_zev.get((m - 1, hf), 0))
                    for kk in range(16):
                        for nb2 in range(2):
                            p.op("te", lambda e, bk0=bk0, kk=kk, nb2=nb2,
                                 hf=hf, m=m: e.matmul(
                                     bank(bk0 + nb2),
                                     wob[m % 2][:, kk * 128:(kk + 1) * 128],
                                     A[0:64, attn_off(kk) + hf * 1024 +
                                       nb2 * 512:attn_off(kk) + hf * 1024 +
                                       nb2 * 512 + 512],
                                     start=(kk == 0), stop=(kk == 15)),
                                 sig=(1 if kk == 15 and nb2 == 1 else 0))
                    t_zmm[(m, hf)] = p.cnt["te"]
                    p.wait("s", "te", t_zmm[(m, hf)])
                    if m >= 1:
                        p.wait("s", "te", t_zst.get((m - 1, hf), 0))
                    p.op("s", lambda e, bk0=bk0, m=m, hf=hf: e.activation(
                        abf(16384 + m * 2048 + hf * 1024, 1024), bank(bk0, 2),
                        AF.Identity, bias=bias_t[:, 56 + m:57 + m]))
                    t_zev[(m, hf)] = p.op("s", lambda e, bk0=bk0, m=m, hf=hf:
                        e.activation(z2b[hf], bank(bk0, 2), AF.Square,
                                     bias=bias_t[:, 56 + m:57 + m]), sig=1)
                    p.wait("te", "s", t_zev[(m, hf)])
                    for nb2 in range(2):
                        p.op("te", lambda e, hf=hf, nb2=nb2, m=m: e.matmul(
                            bank(4 + 2 * hf + nb2)[0:1, :], onesb[:, 0:1],
                            z2b[hf][:, nb2 * 512:(nb2 + 1) * 512],
                            start=(m == 0), stop=(m == NC - 1)),
                            sig=(1 if nb2 == 1 else 0))
                    t_zst[(m, hf)] = p.cnt["te"]
            t_zstat = p.cnt["te"]
            p.wait("s", "te", t_zstat)
            p.op("s", lambda e: e.activation(
                rrow(32, 0, 2048), bank(4, 4)[0:1, :], AF.Ln,
                bias=RMS_EPS, scale=1.0 / D))
            t_inv4 = p.op("s", lambda e: e.activation(
                rowr[0:1, 0:2048], rrow(32, 0, 2048), AF.Exp,
                scale=-0.5), sig=1)
            p.wait("te", "s", t_inv4)
            for nb in range(4):
                p.op("te", lambda e, nb=nb: e.matmul(
                    bank(nb), onesr[0:1, :], rrow_r(0, nb),
                    start=True, stop=True), sig=(1 if nb == 3 else 0))
            t_invb2 = p.cnt["te"]
            p.wait("v", "te", t_invb2)
            t1z = af32(49152, 512)
            for cc2 in range(8):
                for nb in range(4):
                    zs = abf(16384 + cc2 * 2048 + nb * 512, 512)
                    xs = xres[:, cc2 * T + nb * 512:cc2 * T + (nb + 1) * 512]
                    p.op("v", lambda e, zs=zs, nb=nb, cc2=cc2:
                         e.scalar_tensor_tensor(
                             t1z, zs, bias_t[:, 16 + cc2:17 + cc2], bank(nb),
                             ALU.mult, ALU.mult))
                    p.op("v", lambda e, xs=xs: e.tensor_add(xs, xs, t1z),
                         sig=(1 if (cc2 == 7 and nb == 3) else 0))
            p.barrier(skip=("g",))

        if stage >= 2:
            emit_attn()

        if stage >= 3:
            emit_ffn()

        out0 = af32(49152, 1024)
        out1 = af32(51200, 1024)
        t_evo, t_sto = {}, {}
        for lc in range(16):
            buf = out0 if lc % 2 == 0 else out1
            for c in range(NC):
                bk = (lc * NC + c) % 2
                if (lc * NC + c) >= 2:
                    p.wait("te", "s", t_evo[(lc * NC + c) - 2])
                p.op("te", lambda e, bk=bk, c=c, lc=lc: e.transpose(
                    bank(bk)[0:128, 0:128],
                    xres[:, c * T + lc * 128:c * T + (lc + 1) * 128],
                    ident_t[:]), sig=1)
                t_mm = p.cnt["te"]
                if lc >= 2 and c == 0:
                    p.waitt("s", t_sto[lc - 2])
                p.wait("s", "te", t_mm)
                t_evo[lc * NC + c] = p.op("s", lambda e, bk=bk, buf=buf, c=c:
                    e.activation(buf[:, c * 128:(c + 1) * 128],
                                 bank(bk)[0:128, 0:128], AF.Copy), sig=1)
            p.wait("sy", "s", t_evo[lc * NC + NC - 1])
            t_sto[lc] = dma("sy", y_out[lc * 128:(lc + 1) * 128, :], buf, ch=lc)
        p.q["sy"].append(lambda e, t=p.cnt["syA"]: e.wait_ge(p.sem["syA"], t))
        p.q["sy"].append(lambda e, t=p.cnt["syB"]: e.wait_ge(p.sem["syB"], t))

        blk = e_(nc.Block())
        p.run(blk)
        return nc, dbg_outs


def _qperm():
    perm = np.zeros(D, np.int64)
    for c in range(8):
        for pp in range(128):
            h = 4 * (c % 4) + pp // 32
            j = pp % 32
            perm[c * 128 + pp] = 64 * h + 2 * j + (1 if c >= 4 else 0)
    return perm


def _kperm():
    perm = np.zeros(D, np.int64)
    for h in range(NH):
        for r in range(64):
            perm[64 * h + r] = 64 * h + (2 * r if r < 32 else 2 * (r - 32) + 1)
    return perm


def _host_prepare(inputs):
    x = np.asarray(inputs["L0_attn_x"], np.float32)
    wqkv = np.asarray(inputs["L0_attn_Wqkv"], np.float32)
    bqkv = np.asarray(inputs["L0_attn_bqkv"], np.float32)
    gq = np.asarray(inputs["L0_attn_gq"], np.float32)
    gk = np.asarray(inputs["L0_attn_gk"], np.float32)
    wout = np.asarray(inputs["L0_attn_Wout"], np.float32)
    bout = np.asarray(inputs["L0_attn_bout"], np.float32)
    gout = np.asarray(inputs["L0_attn_gout"], np.float32)
    wfc = np.asarray(inputs["L0_ffn_W_fc"], np.float32)
    bfc = np.asarray(inputs["L0_ffn_b_fc"], np.float32)
    afc = np.asarray(inputs["L0_ffn_A_fc"], np.float32)
    bfcm = np.asarray(inputs["L0_ffn_B_fc"], np.float32)
    wpj = np.asarray(inputs["L0_ffn_W_proj"], np.float32)
    bpj = np.asarray(inputs["L0_ffn_b_proj"], np.float32)
    apj = np.asarray(inputs["L0_ffn_A_proj"], np.float32)
    bpjm = np.asarray(inputs["L0_ffn_B_proj"], np.float32)

    qp = _qperm()
    kp = _kperm()
    wqkv_T = np.ascontiguousarray(wqkv.T)
    wqkv_p = np.concatenate([wqkv_T[:, 0:D][:, qp], wqkv_T[:, D:2 * D][:, kp],
                             wqkv_T[:, 2 * D:3 * D]], axis=1)
    wqkv_tiles = np.ascontiguousarray(
        wqkv_p.reshape(NC, 128, 24, 128).transpose(2, 0, 1, 3))
    wout_T = np.ascontiguousarray(wout.T)
    wout_tiles = np.ascontiguousarray(
        wout_T.reshape(16, 64, NC, 128).transpose(2, 0, 1, 3))

    bq_p = bqkv[0:D][qp]
    bk_p = bqkv[D:2 * D][kp]
    bv = bqkv[2 * D:3 * D]
    bias_cols = np.zeros((128, 72), np.float32)
    bias_cols[:, 0:8] = bq_p.reshape(8, 128).T
    bias_cols[:, 8:16] = gq[qp].reshape(8, 128).T
    bias_cols[:, 16:24] = gout.reshape(8, 128).T
    bias_cols[:, 24:56] = bfc.reshape(32, 128).T
    bias_cols[:, 56:64] = bout.reshape(8, 128).T
    bias_cols[:, 64:72] = bpj.reshape(8, 128).T

    rows_f = gk[kp].reshape(1, D)
    rows_b = np.zeros((1, 2048), np.float32)
    rows_b[0, 0:1024] = bk_p
    rows_b[0, 1024:2048] = bv

    j = np.arange(32, dtype=np.float64)
    invf = THETA ** (-j / 32.0)

    def tables(l0):
        t = np.arange(l0, l0 + T, dtype=np.float64)
        ang = np.outer(t, invf)
        cos = np.cos(ang).astype(np.float32)
        sin = np.sin(ang).astype(np.float32)
        cq = np.ascontiguousarray(np.tile(cos.T, (4, 1)))
        sq = np.ascontiguousarray(np.tile(sin.T, (4, 1)))
        ctm = np.zeros((128, 512), np.float32)
        stm = np.zeros((128, 512), np.float32)
        for lc in range(16):
            ctm[:, lc * 32:(lc + 1) * 32] = cos[lc * 128:(lc + 1) * 128, :]
            stm[:, lc * 32:(lc + 1) * 32] = sin[lc * 128:(lc + 1) * 128, :]
        return cq, sq, ctm, stm

    shared = {
        "wqkv_t": wqkv_tiles, "wout_t": wout_tiles,
        "w1t": np.ascontiguousarray(wfc.T), "w2t": np.ascontiguousarray(wpj.T),
        "a_fc": afc, "b_fct": np.ascontiguousarray(bfcm.T),
        "a_pj": apj, "b_pjt": np.ascontiguousarray(bpjm.T),
        "bias_cols": bias_cols, "rows_f": rows_f, "rows_b": rows_b,
        "identf": np.eye(128, dtype=np.float32),
    }
    in_maps = []
    for c in range(8):
        b = c // 2
        l0 = (c % 2) * T
        cq, sq, ctm, stm = tables(l0)
        m = dict(shared)
        m["x"] = np.ascontiguousarray(x[b, l0:l0 + T, :])
        m["cosq"], m["sinq"], m["costm"], m["sintm"] = cq, sq, ctm, stm
        in_maps.append(m)
    return in_maps


def kernel(**inputs):
    key = (_STAGE, tuple(sorted(_DBG)))
    if key not in _CACHE:
        _CACHE[key] = _build(_STAGE)
    nc, dbg_outs = _CACHE[key]
    in_maps = _host_prepare(inputs)
    res = run_bass_kernel_spmd(nc, in_maps, core_ids=list(range(8)))
    out = np.zeros((Bb, Ls, D), np.float32)
    for c in range(8):
        out[c // 2, (c % 2) * T:(c % 2 + 1) * T, :] = res.results[c]["y"]
    if _DBG:
        kernel._dbg_results = [
            {k: res.results[c].get("dbg_" + k) for k in _DBG} for c in range(8)]
    return out
